# version 42
# baseline (speedup 1.0000x reference)
import numpy as np
import ml_dtypes
import concourse.bass as bass
import concourse.mybir as mybir
from concourse.bass_utils import run_bass_kernel_spmd

F32 = mybir.dt.float32
BF16 = mybir.dt.bfloat16
AF = mybir.ActivationFunctionType
ALU = mybir.AluOpType
AX = mybir.AxisListType
BF = ml_dtypes.bfloat16

LNEPS = 1e-5
RMSEPS = 1e-6
TOPEPS = 1e-8
BLK = 256
TTW = 512


class Cfg:
    def __init__(self, D, H, NT, DFF, chunk=1024):
        self.D = D
        self.H = H
        self.DH = 128
        self.NT = NT
        self.DFF = DFF
        self.E = 8
        self.R = 8
        self.LR = 72
        self.chunk = min(chunk, NT)
        assert self.chunk % TTW == 0 and NT % self.chunk == 0
        self.NC = NT // self.chunk
        self.TT = self.chunk // TTW
        self.NB = self.chunk // BLK
        self.NG = self.chunk // 128
        self.FC = D // 128
        self.DC = DFF // 128
        assert D % 128 == 0 and DFF % 128 == 0 and H % 4 == 0
        assert D == H * 128


def _tile_w(wt, pw):
    F, O = wt.shape
    a = wt.reshape(F // 128, 128, O // pw, pw).transpose(2, 1, 0, 3)
    return np.ascontiguousarray(a, dtype=np.float32)


def _pack_lora(wt, bmat, pw=128):
    F, O = wt.shape
    a = _tile_w(wt, pw)
    bpad = np.zeros((O // pw, 128, 1, pw), np.float32)
    b = bmat.reshape(72, O // pw, pw).transpose(1, 0, 2)
    bpad[:, :72, 0, :] = b
    return np.ascontiguousarray(np.concatenate([a, bpad], axis=2))


def prep_inputs(cfg, ins):
    c = cfg
    f32 = np.float32
    wqkv = np.asarray(ins["L0_attn_w_qkv"], f32)
    wqkvT = np.ascontiguousarray(wqkv.T)
    D, H = c.D, c.H
    qk = np.empty((H, 128, c.FC, 256), f32)
    for h in range(H):
        qh = wqkvT[:, h * 128:(h + 1) * 128]
        kh = wqkvT[:, D + h * 128: D + (h + 1) * 128]
        pan = np.concatenate([qh, kh], axis=1)
        qk[h] = pan.reshape(c.FC, 128, 256).transpose(1, 0, 2)
    wv = np.empty((H // 4, 128, c.FC, 512), f32)
    vT = wqkvT[:, 2 * D:3 * D]
    for g in range(H // 4):
        pan = vT[:, g * 512:(g + 1) * 512]
        wv[g] = pan.reshape(c.FC, 128, 512).transpose(1, 0, 2)

    wg = _tile_w(np.ascontiguousarray(np.asarray(ins["L0_attn_w_gate"], f32).T), 256)
    wo = _tile_w(np.ascontiguousarray(np.asarray(ins["L0_attn_w_out"], f32).T), 256)

    def lstack(A, sA):
        A = np.asarray(A, f32); sA = np.asarray(sA, f32)
        a = A.transpose(1, 0, 2).reshape(A.shape[1], 64)
        st = np.concatenate([a, sA], axis=1)
        return np.ascontiguousarray(
            st.reshape(st.shape[0] // 128, 128, 72).transpose(1, 0, 2))

    def bstack(B, sB):
        B = np.asarray(B, f32).reshape(64, -1)
        return np.concatenate([B, np.asarray(sB, f32)], axis=0)

    wgate = _pack_lora(np.asarray(ins["L0_ffn_W_gate"], f32),
                       bstack(ins["L0_ffn_B_gate"], ins["L0_ffn_sB_gate"]))
    wup = _pack_lora(np.asarray(ins["L0_ffn_W_up"], f32),
                     bstack(ins["L0_ffn_B_up"], ins["L0_ffn_sB_up"]))
    wdown = _pack_lora(np.asarray(ins["L0_ffn_W_down"], f32),
                       bstack(ins["L0_ffn_B_down"], ins["L0_ffn_sB_down"]))

    wr = np.ascontiguousarray(
        np.asarray(ins["L0_ffn_W_router"], f32).reshape(c.FC, 128, 8).transpose(1, 0, 2))

    nw = np.asarray(ins["L0_attn_norm_w"], f32).reshape(c.FC, 128).T

    i0 = np.arange(BLK, dtype=f32)
    posm = (i0[None, :] - i0[:, None])
    posm = np.where(posm >= 0, posm, 1e9).astype(f32)
    posm = np.ascontiguousarray(posm.reshape(2, 128, BLK).transpose(1, 0, 2))
    rowi = np.broadcast_to(i0[None, :] + 1.0, (128, BLK)).astype(f32)
    rowkd = np.empty((128, 2), f32)
    for par in range(2):
        rowkd[:, par] = 255.0 - (par * 128 + np.arange(128, dtype=f32))
    e8 = np.zeros((8, 64), f32)
    for e in range(8):
        e8[e, e * 8:(e + 1) * 8] = 0.125
    shared = {
        "wqk": qk, "wv": wv, "wg": wg, "wo": wo,
        "wgate": wgate, "wup": wup, "wdown": wdown, "wr": wr,
        "lga": lstack(ins["L0_ffn_A_gate"], ins["L0_ffn_sA_gate"]),
        "lua": lstack(ins["L0_ffn_A_up"], ins["L0_ffn_sA_up"]),
        "lda": lstack(ins["L0_ffn_A_down"], ins["L0_ffn_sA_down"]),
        "slope": np.asarray(ins["L0_attn_slope_rate"], f32).reshape(1, H),
        "ebias": np.asarray(ins["L0_ffn_expert_bias"], f32).reshape(1, 8),
        "normw": np.ascontiguousarray(nw),
        "idnb": np.eye(128, dtype=f32).astype(BF),
        "idnf": np.eye(128, dtype=f32),
        "onesb": np.ones((128, 128), f32).astype(BF),
        "onesf": np.ones((1, 128), f32),
        "rowi": rowi, "posm": posm, "rowkd": rowkd,
        "e8": e8.astype(BF),
    }
    x = np.asarray(ins["L0_attn_hidden_states"], f32)
    xs = []
    for b in range(x.shape[0]):
        xt = np.ascontiguousarray(x[b].T)
        xt = xt.reshape(c.FC, 128, c.NC, c.TT, TTW).transpose(2, 3, 1, 0, 4)
        xs.append(np.ascontiguousarray(xt))
    return shared, xs


def unshard_out(cfg, outs):
    c = cfg
    res = []
    for o in outs:
        a = o.transpose(3, 2, 0, 1, 4).reshape(c.D, c.NT)
        res.append(a.T)
    return np.ascontiguousarray(np.stack(res)).astype(np.float32)


class Buf:
    __slots__ = ("w", "r", "name")

    def __init__(self, name=""):
        self.name = name
        self.w = None
        self.r = []


class Prog:
    ENGS = ("pe", "act", "dve", "sp", "gp")

    def __init__(self, nc):
        self.nc = nc
        self.stream = {e: [] for e in self.ENGS}
        self.semh = {s: nc.alloc_semaphore("S" + s)
                     for s in ("pe", "act", "dve", "gpc")}
        self.cnt = {s: 0 for s in self.semh}
        self.hw = {}
        self.esem = {"pe": "pe", "act": "act", "dve": "dve", "gp": "gpc"}

    def dsem(self, name):
        if name not in self.semh:
            self.semh[name] = self.nc.alloc_semaphore("D" + name)
            self.cnt[name] = 0
        return name

    def step(self, eng, fn, reads=(), writes=(), dma=None, ndma=1, aliases=()):
        toks = []
        for b in reads:
            if b.w is not None:
                toks.append(b.w)
        for b in writes:
            if b.w is not None:
                toks.append(b.w)
            toks.extend(b.r)
        for b in aliases:
            if b.w is not None:
                toks.append(b.w)
            toks.extend(b.r)
        need = {}
        for s, v in toks:
            if need.get(s, -1) < v:
                need[s] = v
        waits = []
        for s, v in need.items():
            key = (eng, s)
            if self.hw.get(key, 0) >= v:
                continue
            self.hw[key] = v
            waits.append((s, v))
        if dma is not None:
            sname = self.dsem(dma)
            amt = 16
            self.cnt[sname] += 16 * ndma
        else:
            sname = self.esem[eng]
            amt = 1
            self.cnt[sname] += 1
        tok = (sname, self.cnt[sname])
        self.stream[eng].append((waits, fn, sname, amt))
        for b in reads:
            b.r.append(tok)
        for b in writes:
            b.w = tok
            b.r = []
        for b in aliases:
            b.r.append(tok)
        return tok

    def emit(self):
        nc = self.nc
        semh = self.semh
        stream = self.stream
        finals = dict(self.cnt)

        with nc.Block() as block:
            def make(key):
                def body(e):
                    for waits, fn, sname, amt in stream[key]:
                        for s, v in waits:
                            e.wait_ge(semh[s], v)
                        last = fn(e)
                        if isinstance(last, list):
                            for ins in last:
                                ins.then_inc(semh[sname], amt)
                        else:
                            assert last is not None, f"step on {key} -> None"
                            last.then_inc(semh[sname], amt)
                    if key == "sp":
                        for s, v in finals.items():
                            if v > self.hw.get(("sp", s), 0):
                                e.wait_ge(semh[s], v)
                return body
            block.tensor(make("pe"))
            block.scalar(make("act"))
            block.vector(make("dve"))
            block.sync(make("sp"))
            block.gpsimd(make("gp"))


def build_nc(cfg):
    c = cfg
    nc = bass.Bass("TRN2", target_bir_lowering=False)
    p = Prog(nc)

    def din(name, shape, dt=F32):
        return nc.dram_tensor(name, list(shape), dt, kind="ExternalInput").ap()

    xt = din("xt", (c.NC, c.TT, 128, c.FC, TTW))
    wqk = din("wqk", (c.H, 128, c.FC, 256))
    wv = din("wv", (c.H // 4, 128, c.FC, 512))
    wg = din("wg", (c.FC // 2, 128, c.FC, 256))
    wo = din("wo", (c.FC // 2, 128, c.FC, 256))
    wgate = din("wgate", (c.DC, 128, c.FC + 1, 128))
    wup = din("wup", (c.DC, 128, c.FC + 1, 128))
    wdown = din("wdown", (c.FC, 128, c.DC + 1, 128))
    wr = din("wr", (128, c.FC, 8))
    lga = din("lga", (128, c.FC, 72))
    lua = din("lua", (128, c.FC, 72))
    lda = din("lda", (128, c.DC, 72))
    slope = din("slope", (1, c.H))
    ebias = din("ebias", (1, 8))
    normw = din("normw", (128, c.FC))
    idnb = din("idnb", (128, 128), BF16)
    idnf = din("idnf", (128, 128))
    onesb = din("onesb", (128, 128), BF16)
    onesf = din("onesf", (1, 128))
    rowi = din("rowi", (128, BLK))
    posm = din("posm", (128, 2, BLK))
    rowkd = din("rowkd", (128, 2))
    e8 = din("e8", (8, 64), BF16)

    out = nc.dram_tensor("out", [c.NC, c.TT, 128, c.FC, TTW], F32,
                         kind="ExternalOutput").ap()
    tabs = nc.dram_tensor("tabs", [c.H, 128, 3, BLK], BF16, kind="Internal").ap()
    wqks = nc.dram_tensor("wqks", [c.H, 128, c.FC, 256], BF16, kind="Internal").ap()
    wvs = nc.dram_tensor("wvs", [c.H // 4, 128, c.FC, 512], BF16, kind="Internal").ap()
    wgs = nc.dram_tensor("wgs", [c.FC // 2, 128, c.FC, 256], BF16, kind="Internal").ap()
    wos = nc.dram_tensor("wos", [c.FC // 2, 128, c.FC, 256], BF16, kind="Internal").ap()
    wgates = nc.dram_tensor("wgates", [c.DC, 128, c.FC + 1, 128], BF16, kind="Internal").ap()
    wups = nc.dram_tensor("wups", [c.DC, 128, c.FC + 1, 128], BF16, kind="Internal").ap()
    wdowns = nc.dram_tensor("wdowns", [c.FC, 128, c.DC + 1, 128], BF16, kind="Internal").ap()
    x1s = nc.dram_tensor("x1s", [c.NC, c.TT, 128, c.FC, TTW], F32,
                         kind="Internal").ap()

    sb = {}
    bufs = {}

    def S(name, shape, dt=F32, offset=None):
        if offset is None:
            h = nc.alloc_sbuf_tensor(name, list(shape), dt)
        else:
            h = nc.alloc_sbuf_tensor_at(name, list(shape), dt, offset=offset)
        sb[name] = h
        bufs[name] = Buf(name)
        return h

    S("QF0", (128, c.chunk), BF16)
    S("QD0", (128, c.chunk), BF16)
    S("KF0", (128, c.chunk), BF16)
    S("KD0", (128, c.NG, 128), BF16)
    S("QKD", (128, 2, BLK), BF16)
    S("QKD2", (128, 2, BLK), BF16)
    S("QKD3", (128, 2, BLK), BF16)
    S("KVB", (128, c.H, 128), BF16)
    S("DT0", (128, 2, BLK), BF16)
    S("DT1", (128, 2, BLK), BF16)
    S("QDC0", (128, BLK), BF16)
    S("QDC1", (128, BLK), BF16)
    S("KDC", (128, c.H, 2))
    S("SB", (128, c.H))
    S("BD", (128, c.H))
    S("BIAS", (128, 8))
    S("NW", (128, c.FC))
    S("ONESB", (128, 128), BF16)
    S("IDNB", (128, 128), BF16)
    S("IDNF", (128, 128))
    S("ONESF", (1, 128))
    S("ROWKD", (128, 2))
    S("ROWA", (128, 4, TTW))
    S("MBF", (128, TTW), BF16)
    S("RBF", (128, TTW), BF16)
    S("XSQ0", (128, TTW), BF16)
    S("XSQ1", (128, TTW), BF16)
    S("XSQ2", (128, TTW), BF16)
    S("E8", (8, 64), BF16)
    S("SLOPE", (1, c.H))
    S("EPSC", (128, 2))
    S("FEN", (128, 1))
    S("EBIAS", (1, 8))
    S("WRB", (128, c.FC, 8), BF16)
    S("WREP", (64, c.TT, TTW), BF16)
    S("MG", (72, c.TT, TTW), BF16)
    S("MU", (72, c.TT, TTW), BF16)
    S("MD", (72, c.TT, TTW), BF16)
    S("SC", (8, c.chunk), BF16)
    S("SCT", (128, c.TT, TTW // 128, 8))
    S("BSD", (128, TTW // 128, 8))
    S("WTM", (128, TTW // 128, 8))
    S("TOP", (128, TTW // 128, 8))
    S("M1", (128, TTW // 128))
    S("WETB", (8, TTW), BF16)
    S("HS0", (128, TTW), BF16)
    S("HS1", (128, TTW), BF16)
    S("STG0", (128, TTW))
    S("STG1", (128, TTW))
    S("STG2", (128, TTW))
    S("STG3", (128, TTW))
    S("LNB", (128, c.FC, c.TT, TTW), BF16)
    wten = [
        ("WQKB", (128, c.FC, 256), 0),
        ("WQKB1", (128, c.FC, 256), 3 * c.FC * 512),
        ("WVB0", (128, c.FC, 512), c.FC * 512),
        ("WB0", (128, c.FC, 256), 0),
        ("WB1", (128, c.FC, 256), c.FC * 512),
        ("WB2", (128, c.FC, 256), 2 * c.FC * 512),
        ("WB3", (128, c.FC, 256), 3 * c.FC * 512),
        ("WGT0", (128, c.FC + 1, 128), 0),
        ("WGT1", (128, c.FC + 1, 128), (c.FC + 1) * 256),
        ("WUP0", (128, c.FC + 1, 128), 2 * (c.FC + 1) * 256),
        ("WUP1", (128, c.FC + 1, 128), 3 * (c.FC + 1) * 256),
        ("WDN0", (128, c.DC + 1, 128), 0),
        ("WDN1", (128, c.DC + 1, 128), (c.DC + 1) * 256),
        ("LGA", (128, c.FC, 72), 0),
        ("LUA", (128, c.FC, 72), c.FC * 144),
        ("LDA", (128, c.DC, 72), 4 * (c.FC + 1) * 256),
    ]
    wa_bytes = max(o + int(np.prod(sh[1:])) * 2 for _, sh, o in wten)
    wa = nc.alloc_sbuf_tensor("WARENA", [128, wa_bytes], mybir.dt.uint8)
    wabase = nc.lookup_mloc(wa).addr
    wrange = {}
    for nm, sh, o in wten:
        sb[nm] = nc.alloc_sbuf_tensor_at(nm, list(sh), BF16, offset=wabase + o)
        bufs[nm] = Buf(nm)
        wrange[nm] = (o, o + int(np.prod(sh[1:])) * 2)
    woverlap = {}
    for nm, (a0, a1) in wrange.items():
        woverlap[nm] = [bufs[m] for m, (b0, b1) in wrange.items()
                        if m != nm and a0 < b1 and b0 < a1]

    hid_bytes = c.DC * c.chunk * 2
    tenants = [
        ("GT", (128, c.FC, TTW), BF16, None),
        ("ATT", (128, c.H, c.TT, TTW), BF16, None),
        ("X1B", (128, c.FC, TTW), BF16, None),
        ("VTM", (128, c.NG, 512), BF16, None),
        ("QF1", (128, c.chunk), BF16, None),
        ("QD1", (128, c.chunk), BF16, None),
        ("KF1", (128, c.chunk), BF16, None),
        ("KD1", (128, c.NG, 128), BF16, None),
        ("ROWI", (128, BLK), F32, None),
        ("POSM", (128, 2, BLK), F32, None),
    ]
    ten_bytes = 0
    for name, shape, dt, alias in tenants:
        if alias is None:
            n = int(np.prod(shape[1:])) * mybir.dt.size(dt)
            ten_bytes += (n + 31) // 32 * 32
    region_bytes = max(hid_bytes, ten_bytes)
    reg = nc.alloc_sbuf_tensor("OVL", [128, region_bytes], mybir.dt.uint8)
    rbase = nc.lookup_mloc(reg).addr
    cur = rbase
    offs = {}
    for name, shape, dt, alias in tenants:
        if alias is not None:
            offs[name] = offs[alias]
        else:
            offs[name] = cur
            n = int(np.prod(shape[1:])) * mybir.dt.size(dt)
            cur += (n + 31) // 32 * 32
    for name, shape, dt, alias in tenants:
        h = nc.alloc_sbuf_tensor_at(name, list(shape), dt, offset=offs[name])
        sb[name] = h
        bufs[name] = bufs[alias] if alias is not None else Buf(name)
    S("HID", (128, c.DC, c.TT, TTW), BF16, offset=rbase)
    tenant_bufs = [bufs[n] for n, _, _, a in tenants if a is None]

    ps = {}
    for name in ("psA", "psB", "psC", "psD"):
        ps[name] = nc.alloc_psum_tensor(name, [128, TTW], F32)
        bufs[name] = Buf(name)
    ps["psQK"] = nc.alloc_psum_tensor("psQK", [128, TTW], F32)
    bufs["psQK"] = Buf("psQK")
    ps["psAO"] = nc.alloc_psum_tensor("psAO", [128, BLK], F32)
    bufs["psAO"] = Buf("psAO")
    ps["psT"] = nc.alloc_psum_tensor("psT", [128, c.NG, 128], BF16)
    bufs["psT"] = Buf("psT")
    ps["psKV"] = nc.alloc_psum_tensor("psKV", [128, 128], F32)
    bufs["psKV"] = Buf("psKV")

    B = bufs
    T = sb

    def ap(name):
        return T[name][:]

    kvb_b = [Buf(f"kvb{h}") for h in range(c.H)]
    lnbb = [Buf(f"lnb{t}") for t in range(c.TT)]
    attcb = [[Buf(f"att{t}_{o}") for o in range(c.FC)] for t in range(c.TT)]
    x1cb = [Buf(f"x1c{o}") for o in range(c.FC)]
    lnxb = [[Buf(f"lnx{t}_{s}") for s in range(4)] for t in range(c.TT)]
    mgb = [Buf(f"mg{t}") for t in range(c.TT)]
    mub = [Buf(f"mu{t}") for t in range(c.TT)]
    qftb = [[Buf(f"qf{s}_{t}") for t in range(c.TT)] for s in range(2)]
    kftb = [[Buf(f"kf{s}_{t}") for t in range(c.TT)] for s in range(2)]
    qkf_bufs = [b_ for sl in qftb + kftb for b_ in sl]

    def dma(eng, outap, inap, sem, reads=(), writes=(), aliases=()):
        return p.step(eng, lambda e, o=outap, i=inap: e.dma_start(out=o, in_=i),
                      reads=reads, writes=writes, dma=sem, aliases=aliases)

    cinit0 = [("ONESB", onesb), ("ONESF", onesf), ("SLOPE", slope),
              ("IDNB", idnb)]
    cinit = [("IDNF", idnf), ("ROWI", rowi), ("POSM", posm),
             ("ROWKD", rowkd), ("E8", e8), ("EBIAS", ebias), ("NW", normw)]

    def cinit0_fn(e):
        return [e.dma_start(out=ap(nm), in_=src) for nm, src in cinit0]
    p.step("sp", cinit0_fn, writes=[B[nm] for nm, _ in cinit0],
           dma="cinit0", ndma=len(cinit0))

    def cinit_fn(e):
        return [e.dma_start(out=ap(nm), in_=src) for nm, src in cinit]
    p.step("sp", cinit_fn, writes=[B[nm] for nm, _ in cinit],
           dma="cinit", ndma=len(cinit))

    ginit = [("WRB", wr)]

    def ginit_fn(e):
        return [e.dma_start(out=ap(nm), in_=src) for nm, src in ginit]
    p.step("gp", ginit_fn, writes=[B[nm] for nm, _ in ginit],
           dma="ginit", ndma=len(ginit))

    p.step("pe", lambda e: e.matmul(ps["psA"][:, 0:c.H], T["ONESF"][:],
                                    T["SLOPE"][:], start=True, stop=True),
           reads=[B["ONESF"], B["SLOPE"]], writes=[B["psA"]])
    p.step("dve", lambda e: e.tensor_copy(ap("SB"), ps["psA"][:, 0:c.H]),
           reads=[B["psA"]], writes=[B["SB"]])
    p.step("pe", lambda e: e.matmul(ps["psB"][:, 0:8], T["ONESF"][:],
                                    T["EBIAS"][:], start=True, stop=True),
           reads=[B["ONESF"], B["EBIAS"]], writes=[B["psB"]])
    p.step("dve", lambda e: e.tensor_copy(ap("BIAS"), ps["psB"][:, 0:8]),
           reads=[B["psB"]], writes=[B["BIAS"]])
    p.step("act", lambda e: e.activation(ap("BD"), ap("SB"), AF.Exp,
                                         scale=-float(BLK)),
           reads=[B["SB"]], writes=[B["BD"]])
    rowa = T["ROWA"]
    for h in range(c.H):
        p.step("dve", lambda e, h=h: e.tensor_scalar_mul(
            rowa[:, 0, 0:2], T["ROWKD"][:], T["SB"][:, h:h + 1]),
            reads=[B["ROWKD"], B["SB"]], writes=[B["ROWA"]])
        p.step("act", lambda e, h=h: e.activation(
            T["KDC"][:, h, :], rowa[:, 0, 0:2], AF.Exp, scale=-1.0),
            reads=[B["ROWA"]], writes=[B["KDC"]])
    tabs_b = [Buf(f"tab{h}") for h in range(c.H)]

    def emit_table_init():
        s0v = T["STG0"][:].rearrange("p (a b) -> p a b", b=BLK)
        for h in range(c.H):
            stg = ("DT0", "DT1")[h % 2]
            stgq = ("QDC0", "QDC1")[h % 2]
            sB_, sQ = B[stg], B[stgq]
            p.step("dve", lambda e, h=h: e.tensor_scalar_mul(
                s0v, T["POSM"][:], T["SB"][:, h:h + 1]),
                reads=[B["POSM"], B["SB"]], writes=[B["STG0"]])
            p.step("act", lambda e, stg=stg: e.activation(
                T[stg][:], s0v, AF.Exp, scale=-1.0),
                reads=[B["STG0"]], writes=[sB_])
            p.step("dve", lambda e, h=h: e.tensor_scalar_mul(
                T["STG1"][:, 0:BLK], T["ROWI"][:], T["SB"][:, h:h + 1]),
                reads=[B["ROWI"], B["SB"]], writes=[B["STG1"]])
            p.step("act", lambda e, stgq=stgq: e.activation(
                T[stgq][:], T["STG1"][:, 0:BLK], AF.Exp, scale=-1.0),
                reads=[B["STG1"]], writes=[sQ])
            p.step("sp", lambda e, h=h, stg=stg, stgq=stgq: [
                e.dma_start(out=tabs[h, :, 0:2, :], in_=T[stg][:]),
                e.dma_start(out=tabs[h, :, 2, :], in_=T[stgq][:])],
                reads=[sB_, sQ], writes=[tabs_b[h]], dma=f"tabst{h % 2}", ndma=2)
    p.step("dve", lambda e: e.memset(ap("KVB"), 0.0), writes=[B["KVB"]] + kvb_b)
    p.step("dve", lambda e: e.memset(T["EPSC"][:, 0:1], LNEPS), writes=[B["EPSC"]])
    p.step("dve", lambda e: e.memset(T["EPSC"][:, 1:2], RMSEPS), writes=[B["EPSC"]])

    def mm_group(psum_ap, pairs, psum_buf, reads, tag=""):
        n = len(pairs)

        def fn(e, pairs=tuple(pairs), psum_ap=psum_ap, n=n):
            last = None
            for i, (l, r) in enumerate(pairs):
                last = e.matmul(psum_ap, l, r, start=(i == 0), stop=(i == n - 1))
            return last
        return p.step("pe", fn, reads=reads, writes=[psum_buf])

    def ln_rows(sum_buf, sum_ap, sq_buf, sq_ap, epsc, with_mean, mrow, rrow):
        inv = 1.0 / c.D
        if with_mean:
            p.step("dve", lambda e: e.tensor_scalar_mul(
                rowa[:, mrow, :], sum_ap, inv),
                reads=[sum_buf], writes=[B["ROWA"]])
            p.step("act", lambda e: e.activation(
                rowa[:, mrow + 1, :], rowa[:, mrow, :], AF.Square),
                reads=[B["ROWA"]], writes=[B["ROWA"]])
            p.step("dve", lambda e: e.tensor_scalar(
                rowa[:, mrow + 2, :], sq_ap, inv, None, ALU.mult),
                reads=[sq_buf], writes=[B["ROWA"]])
            p.step("dve", lambda e: e.tensor_sub(
                rowa[:, mrow + 2, :], rowa[:, mrow + 2, :], rowa[:, mrow + 1, :]),
                reads=[B["ROWA"]], writes=[B["ROWA"]])
            vrow = mrow + 2
        else:
            p.step("dve", lambda e: e.tensor_scalar_mul(
                rowa[:, mrow + 2, :], sq_ap, inv),
                reads=[sq_buf], writes=[B["ROWA"]])
            vrow = mrow + 2
        p.step("act", lambda e, vrow=vrow: e.activation(
            rowa[:, rrow, :], rowa[:, vrow, :], AF.Ln,
            bias=T["EPSC"][:, epsc:epsc + 1]),
            reads=[B["ROWA"], B["EPSC"]], writes=[B["ROWA"]])
        p.step("act", lambda e: e.activation(
            rowa[:, rrow, :], rowa[:, rrow, :], AF.Exp, scale=-0.5),
            reads=[B["ROWA"]], writes=[B["ROWA"]])

    def rows_cast(mrow, rrow, with_mean=True):
        if with_mean:
            p.step("dve", lambda e: e.tensor_copy(ap("MBF"), rowa[:, mrow, :]),
                   reads=[B["ROWA"]], writes=[B["MBF"]])
        p.step("dve", lambda e: e.tensor_copy(ap("RBF"), rowa[:, rrow, :]),
               reads=[B["ROWA"]], writes=[B["RBF"]])

    x1sb = {(tt, oc): Buf(f"x1s{tt}_{oc}")
            for tt in range(c.TT) for oc in range(c.FC)}
    scr_b = {}

    preloaded = set()

    def wload(ci, bufname, src_f32, scr_slice, key, q="gp"):
        if key in preloaded:
            preloaded.discard(key)
            return
        if key not in scr_b:
            scr_b[key] = Buf(f"scr{key}")
        al = woverlap[bufname]
        dstap = ap(bufname)
        n1 = dstap.shape[1]
        ns = 4 if ci == 0 else 2
        bounds = sorted(set(n1 * k // ns for k in range(ns + 1)))
        segs = [(a, b) for a, b in zip(bounds, bounds[1:]) if b > a]
        if ci == 0:
            def ldfn(e, o=dstap, i=src_f32, segs=segs):
                return [e.dma_start(out=o[:, a:b], in_=i[:, a:b])
                        for a, b in segs]
            p.step("gp", ldfn, writes=[B[bufname]], aliases=al, dma=bufname,
                   ndma=len(segs))

            def tefn(e, o=scr_slice, i=dstap, segs=segs):
                return [e.dma_start(out=o[:, a:b], in_=i[:, a:b])
                        for a, b in segs]
            p.step("sp", tefn, reads=[B[bufname]], writes=[scr_b[key]],
                   dma=bufname + "s", ndma=len(segs))
        else:
            def rlfn(e, o=dstap, i=scr_slice, segs=segs):
                return [e.dma_start(out=o[:, a:b], in_=i[:, a:b])
                        for a, b in segs]
            p.step(q, rlfn, reads=[scr_b[key]], writes=[B[bufname]],
                   aliases=al, dma=bufname + ("h" if q == "sp" else ""),
                   ndma=len(segs))
    for ci in range(c.NC):
        p.step("dve", lambda e: e.memset(T["FEN"][:], 0.0),
               writes=[B["HID"], B["FEN"]] + tenant_bufs + qkf_bufs
               + [bb for tl in attcb for bb in tl])
        def ln1_stat_unit(tt, fc, ci=ci):
            xs = ("XSQ0", "XSQ1")[fc % 2]
            slot = lnxb[tt][fc % 4]
            dst = T["LNB"][:, fc, tt, :]
            dma("gp", dst, xt[ci, tt, :, fc, :], f"lnx{tt}_{fc % 4}",
                writes=[slot])
            p.step("act", lambda e, dst=dst, xs=xs: e.activation(
                T[xs][:], dst, AF.Square),
                reads=[slot], writes=[B[xs]])

            def stfn(e, dst=dst, xs=xs, fc=fc):
                e.matmul(ps["psQK"][:], T["ONESB"][:], dst,
                         start=(fc == 0), stop=(fc == c.FC - 1))
                return e.matmul(ps["psC"][:], T["ONESB"][:], T[xs][:],
                                start=(fc == 0), stop=(fc == c.FC - 1))
            p.step("pe", stfn, reads=[slot, B[xs], B["ONESB"]],
                   writes=[B["psQK"], B["psC"]])

        def ln1_rows(tt, ci=ci):
            ln_rows(B["psQK"], ps["psQK"][:], B["psC"], ps["psC"][:],
                    0, True, mrow=0, rrow=3)
            p.step("dve", lambda e: e.memset(T["FEN"][:], 0.0),
                   reads=lnxb[tt], writes=[lnbb[tt], B["FEN"]])
            rows_cast(0, 3)

        def ln1_apply_piece(tt, a, ci=ci):
            fq = c.FC // 4
            dst = T["LNB"][:, a * fq:(a + 1) * fq, tt, :]
            mb = T["MBF"][:].unsqueeze(1).broadcast_to([128, fq, TTW])
            rb = T["RBF"][:].unsqueeze(1).broadcast_to([128, fq, TTW])
            p.step("dve", lambda e, dst=dst, mb=mb: e.tensor_tensor(
                dst, dst, mb, ALU.subtract),
                reads=[B["MBF"], lnbb[tt]], writes=[lnbb[tt]])
            p.step("dve", lambda e, dst=dst, rb=rb: e.tensor_tensor(
                dst, dst, rb, ALU.mult),
                reads=[B["RBF"], lnbb[tt]], writes=[lnbb[tt]])

        def ln1_apply(tt, ci=ci):
            ln1_rows(tt, ci)
            for a in range(4):
                ln1_apply_piece(tt, a, ci)
        if ci == 0:
            for pk, bn, sf, ss in ((("wqk", 0), "WQKB1", wqk[0], wqks[0]),
                                   (("wv", 0), "WVB0", wv[0], wvs[0]),
                                   (("wqk", 1), "WQKB", wqk[1], wqks[1])):
                wload(0, bn, sf, ss, pk)
                preloaded.add(pk)
            for tt_ in range(c.TT):
                for fc_ in range(c.FC):
                    ln1_stat_unit(tt_, fc_, 0)
                ln1_apply(tt_, 0)
            emit_table_init()


        def vproj(g):
            wload(ci, "WVB0", wv[g], wvs[g], ("wv", g))
            wvb = T["WVB0"]
            wvbB = B["WVB0"]
            for tc in range(c.NG):
                tt0, o0 = divmod(tc * 128, TTW)
                pairs = [(T["LNB"][:, fc, tt0, o0:o0 + 128], wvb[:, fc, :])
                         for fc in range(c.FC)]
                pb = "psA" if tc % 2 == 0 else "psB"
                mm_group(ps[pb][:], pairs, B[pb], [lnbb[tt0], wvbB])
                p.step("act", lambda e, tc=tc, pb=pb: e.activation(
                    T["VTM"][:, tc, :], ps[pb][:], AF.Silu),
                    reads=[B[pb]], writes=[B["VTM"]])

        qkv_banks = [ps["psQK"][:].rearrange("p (a b) -> p a b", b=BLK),
                     ps["psC"][:].rearrange("p (a b) -> p a b", b=BLK)]
        qkv_bufs = [B["psQK"], B["psC"]]
        ao_banks = [ps["psAO"][:], ps["psD"][:, 0:BLK]]
        ao_bufs = [B["psAO"], B["psD"]]
        qkd_names = ["QKD", "QKD2", "QKD3"]

        def emit_qk(h, b):
            s = h % 2
            i0_ = b * BLK
            bank = qkv_banks[b % 2]

            def qkfn(e, b=b, i0_=i0_, bank=bank, s=s):
                e.matmul(bank[:, 0, :],
                         T[f"KF{s}"][:, (2 * b) * 128:(2 * b + 1) * 128],
                         T[f"QF{s}"][:, i0_:i0_ + BLK], start=True, stop=True)
                return e.matmul(bank[:, 1, :],
                                T[f"KF{s}"][:, (2 * b + 1) * 128:(2 * b + 2) * 128],
                                T[f"QF{s}"][:, i0_:i0_ + BLK], start=True, stop=True)
            tti = i0_ // TTW
            p.step("pe", qkfn, reads=[kftb[s][tti], qftb[s][tti]],
                   writes=[qkv_bufs[b % 2]])
            qkd = qkd_names[(h * c.NB + b) % 3]
            dtb_ = ("DT0", "DT1")[s]
            p.step("dve", lambda e, dtb_=dtb_, qkd=qkd, bank=bank:
                   e.tensor_tensor(T[qkd][:], bank, T[dtb_][:], ALU.mult),
                   reads=[qkv_bufs[b % 2], B[dtb_]], writes=[B[qkd]])

        def proj(h):
            s = h % 2
            dtb, qdc = ("DT0", "QDC0") if s == 0 else ("DT1", "QDC1")
            p.step("gp", lambda e, h=h, dtb=dtb, qdc=qdc: [
                e.dma_start(out=T[dtb][:], in_=tabs[h, :, 0:2, :]),
                e.dma_start(out=T[qdc][:], in_=tabs[h, :, 2, :])],
                reads=[tabs_b[h]], writes=[B[dtb], B[qdc]],
                dma=dtb, ndma=2)
            wqn = ("WQKB1", "WQKB")[h % 2]
            wload(ci, wqn, wqk[h], wqks[h], ("wqk", h), q="sp")
            for tt in range(c.TT):
                qpairs = [(T[wqn][:, fc, 0:128], T["LNB"][:, fc, tt, :])
                          for fc in range(c.FC)]
                kpairs = [(T[wqn][:, fc, 128:256], T["LNB"][:, fc, tt, :])
                          for fc in range(c.FC)]
                mm_group(ps["psA"][:], qpairs, B["psA"], [B[wqn], lnbb[tt]])
                mm_group(ps["psB"][:], kpairs, B["psB"], [B[wqn], lnbb[tt]])
                p.step("act", lambda e, tt=tt, s=s: e.activation(
                    T[f"QF{s}"][:, tt * TTW:(tt + 1) * TTW], ps["psA"][:], AF.Silu),
                    reads=[B["psA"]], writes=[qftb[s][tt]],
                    aliases=[B[f"QF{s}"]])
                p.step("act", lambda e, tt=tt, s=s: e.activation(
                    T[f"KF{s}"][:, tt * TTW:(tt + 1) * TTW], ps["psB"][:], AF.Silu),
                    reads=[B["psB"]], writes=[kftb[s][tt]],
                    aliases=[B[f"KF{s}"]])
            qdb = T[qdc][:].unsqueeze(1).broadcast_to([128, c.NB, BLK])
            p.step("dve", lambda e, qdb=qdb, s=s: e.tensor_tensor(
                T[f"QD{s}"][:].rearrange("p (nb i) -> p nb i", i=BLK),
                T[f"QF{s}"][:].rearrange("p (nb i) -> p nb i", i=BLK),
                qdb, ALU.mult),
                reads=qftb[s][:c.TT] + [B[qdc]], writes=[B[f"QD{s}"]])

            def tfn(e, s=s):
                last = None
                for jt in range(c.NG):
                    last = e.transpose(ps["psT"][:, jt, :],
                                       T[f"KF{s}"][:, jt * 128:(jt + 1) * 128],
                                       T["IDNB"][:])
                return last
            p.step("pe", tfn, reads=kftb[s][:c.TT] + [B["IDNB"]],
                   writes=[B["psT"]])
            for par in range(2):
                src_ = ps["psT"][:, par::2, :]
                dst_ = T[f"KD{s}"][:, par::2, :]
                p.step("dve", lambda e, src_=src_, dst_=dst_, h=h, par=par:
                       e.tensor_scalar_mul(dst_, src_, T["KDC"][:, h, par:par + 1]),
                       reads=[B["psT"], B["KDC"]], writes=[B[f"KD{s}"]])
            if h % 4 == 0:
                emit_qk(h, 0)
                if c.NB > 1:
                    emit_qk(h, 1)

        def blocks(h):
            s = h % 2
            j = h % 4
            dtb = ("DT0", "DT1")[s]
            for b in range(c.NB):
                i0_ = b * BLK
                qkd = qkd_names[(h * c.NB + b) % 3]
                aob = ao_banks[b % 2]

                def aofn(e, b=b, i0_=i0_, h=h, j=j, qkd=qkd, aob=aob, s=s):
                    e.matmul(aob, T["KVB"][:, h, :],
                             T[f"QD{s}"][:, i0_:i0_ + BLK], start=True, stop=False)
                    e.matmul(aob, T["VTM"][:, 2 * b, j * 128:(j + 1) * 128],
                             T[qkd][:, 0, :], start=False, stop=False)
                    return e.matmul(aob,
                                    T["VTM"][:, 2 * b + 1, j * 128:(j + 1) * 128],
                                    T[qkd][:, 1, :], start=False, stop=True)
                p.step("pe", aofn, reads=[kvb_b[h], B[f"QD{s}"], B["VTM"], B[qkd]],
                       writes=[ao_bufs[b % 2]])
                tt0 = b // (TTW // BLK)
                half = b % (TTW // BLK)
                p.step("act", lambda e, h=h, tt0=tt0, half=half, aob=aob:
                       e.activation(
                           T["ATT"][:, h, tt0, half * BLK:(half + 1) * BLK],
                           aob, AF.Copy),
                       reads=[ao_bufs[b % 2]], writes=[attcb[tt0][h]])

                def kvfn(e, b=b, j=j, s=s):
                    e.matmul(ps["psKV"][:], T[f"KD{s}"][:, 2 * b, :],
                             T["VTM"][:, 2 * b, j * 128:(j + 1) * 128],
                             start=True, stop=False)
                    return e.matmul(ps["psKV"][:], T[f"KD{s}"][:, 2 * b + 1, :],
                                    T["VTM"][:, 2 * b + 1, j * 128:(j + 1) * 128],
                                    start=False, stop=True)
                p.step("pe", kvfn, reads=[B[f"KD{s}"], B["VTM"]], writes=[B["psKV"]])
                p.step("dve", lambda e, h=h: e.scalar_tensor_tensor(
                    T["KVB"][:, h, :], T["KVB"][:, h, :], T["BD"][:, h:h + 1],
                    ps["psKV"][:], ALU.mult, ALU.add),
                    reads=[B["psKV"], B["BD"], kvb_b[h]], writes=[kvb_b[h]])
                if b + 2 < c.NB:
                    emit_qk(h, b + 2)
                elif h + 1 < c.H and (h + 1) % 4 != 0:
                    emit_qk(h + 1, b + 2 - c.NB)
        proj(0)
        vproj(0)
        for h in range(c.H):
            nh = h + 1
            if nh < c.H and nh % 4 != 0:
                proj(nh)
                blocks(h)
            elif nh < c.H:
                blocks(h)
                proj(nh)
                vproj(nh // 4)
            else:
                blocks(h)

        efg_seq = [0]

        def emit_gate(tt):
            for oc in range(c.FC):
                if oc % 2 == 0:
                    wbn = ("WB0", "WB1", "WB2", "WB3")[efg_seq[0] % 4]
                    efg_seq[0] += 1
                    wload(ci, wbn, wg[oc // 2], wgs[oc // 2], ("wg", oc // 2))
                    emit_gate.cur = wbn
                wbn = emit_gate.cur
                sub = oc % 2
                xs = ("XSQ0", "XSQ1")[oc % 2]
                p.step("act", lambda e, xs=xs, oc=oc, tt=tt: e.activation(
                    T[xs][:], T["ATT"][:, oc, tt, :], AF.Square),
                    reads=[attcb[tt][oc]], writes=[B[xs]])
                pairs = [(T[wbn][:, fc, sub * 128:(sub + 1) * 128],
                          T["LNB"][:, fc, tt, :]) for fc in range(c.FC)]
                pb = ("psA", "psB", "psC")[oc % 3]
                mm_group(ps[pb][:], pairs, B[pb], [B[wbn], lnbb[tt]])
                p.step("pe", lambda e, xs=xs, oc=oc: e.matmul(
                    ps["psD"][:], T["ONESB"][:], T[xs][:],
                    start=(oc == 0), stop=(oc == c.FC - 1)),
                    reads=[B[xs], B["ONESB"]], writes=[B["psD"]])
                p.step("act", lambda e, oc=oc, pb=pb: e.activation(
                    T["GT"][:, oc, :], ps[pb][:], AF.Sigmoid),
                    reads=[B[pb]], writes=[B["GT"]])
                p.step("dve", lambda e, oc=oc, tt=tt: e.scalar_tensor_tensor(
                    T["ATT"][:, oc, tt, :], T["ATT"][:, oc, tt, :],
                    T["NW"][:, oc:oc + 1], T["GT"][:, oc, :],
                    ALU.mult, ALU.mult),
                    reads=[B["GT"], attcb[tt][oc]], writes=[attcb[tt][oc]])

        def emit_rows_p(tt, defer=False):
            p.step("dve", lambda e: e.tensor_scalar_mul(
                rowa[:, 0, :], ps["psD"][:], 1.0 / c.D),
                reads=[B["psD"]], writes=[B["ROWA"]])
            p.step("act", lambda e: e.activation(
                rowa[:, 1, :], rowa[:, 0, :], AF.Ln,
                bias=T["EPSC"][:, 1:2]),
                reads=[B["ROWA"], B["EPSC"]], writes=[B["ROWA"]])
            p.step("act", lambda e: e.activation(
                rowa[:, 1, :], rowa[:, 1, :], AF.Exp, scale=-0.5),
                reads=[B["ROWA"]], writes=[B["ROWA"]])
            rows_cast(0, 1, with_mean=False)
            nq = c.H // 4
            pieces = []
            for a in range(4):
                attv = T["ATT"][:, a * nq:(a + 1) * nq, tt, :]
                rbb = T["RBF"][:].unsqueeze(1).broadcast_to([128, nq, TTW])

                def mk(attv=attv, rbb=rbb, a=a, tt=tt):
                    p.step("dve", lambda e, attv=attv, rbb=rbb: e.tensor_tensor(
                        attv, attv, rbb, ALU.mult), reads=[B["RBF"]],
                        writes=attcb[tt][a * nq:(a + 1) * nq])
                pieces.append(mk)
            if not defer:
                for mk in pieces:
                    mk()
                return []
            return pieces

        def emit_out(tt, inject=(), inj_start=4, inj_step=2, defer_ln2=False):
            inj = list(inject)

            def emit_l2stat(oc):
                xs = ("XSQ0", "XSQ1", "XSQ2")[oc % 3]

                def l2fn(e, oc=oc, xs=xs):
                    e.matmul(ps["psQK"][:], T["ONESB"][:], T["X1B"][:, oc, :],
                             start=(oc == 0), stop=(oc == c.FC - 1))
                    return e.matmul(ps["psD"][:], T["ONESB"][:], T[xs][:],
                                    start=(oc == 0), stop=(oc == c.FC - 1))
                p.step("pe", l2fn, reads=[x1cb[oc], B[xs], B["ONESB"]],
                       writes=[B["psQK"], B["psD"]])
            for oc in range(c.FC):
                if inj and oc >= inj_start and (oc - inj_start) % inj_step == 0:
                    inj.pop(0)()
                if oc % 2 == 0:
                    wbn = ("WB0", "WB1", "WB2", "WB3")[efg_seq[0] % 4]
                    efg_seq[0] += 1
                    wload(ci, wbn, wo[oc // 2], wos[oc // 2], ("wo", oc // 2))
                    emit_out.cur = wbn
                wbn = emit_out.cur
                sub = oc % 2
                stg = ("STG0", "STG1", "STG2", "STG3")[oc % 4]
                if ci == 0:
                    dma("sp", ap(stg), xt[ci, tt, :, oc, :], stg,
                        writes=[B[stg]])
                else:
                    dma("gp", ap(stg), xt[ci, tt, :, oc, :], stg + "g",
                        writes=[B[stg]])
                pairs = [(T[wbn][:, fc, sub * 128:(sub + 1) * 128],
                          T["ATT"][:, fc, tt, :]) for fc in range(c.FC)]
                pb = ("psA", "psB", "psC")[oc % 3]
                mm_group(ps[pb][:], pairs, B[pb], [B[wbn]] + attcb[tt])
                p.step("dve", lambda e, stg=stg, pb=pb: e.tensor_tensor(
                    ap(stg), ap(stg), ps[pb][:], ALU.add),
                    reads=[B[pb], B[stg]], writes=[B[stg]])
                p.step("act", lambda e, stg=stg, oc=oc: e.activation(
                    T["X1B"][:, oc, :], ap(stg), AF.Copy),
                    reads=[B[stg]], writes=[x1cb[oc]])
                xs = ("XSQ0", "XSQ1", "XSQ2")[oc % 3]
                p.step("act", lambda e, stg=stg, xs=xs: e.activation(
                    T[xs][:], ap(stg), AF.Square),
                    reads=[B[stg]], writes=[B[xs]])
                dma("sp", x1s[ci, tt, :, oc, :], ap(stg), stg, reads=[B[stg]],
                    writes=[x1sb[(tt, oc)]])
                if oc > 1:
                    emit_l2stat(oc - 2)
            for fn in inj:
                fn()
            inj = []
            emit_l2stat(c.FC - 2)
            emit_l2stat(c.FC - 1)
            ln_rows(B["psQK"], ps["psQK"][:], B["psD"], ps["psD"][:],
                    0, True, mrow=0, rrow=3)
            rows_cast(0, 3)
            fq = c.FC // 4
            ln2p = []
            for a in range(4):
                dsta = T["LNB"][:, a * fq:(a + 1) * fq, tt, :]
                x1a = T["X1B"][:, a * fq:(a + 1) * fq, :]
                mba = T["MBF"][:].unsqueeze(1).broadcast_to([128, fq, TTW])
                rba = T["RBF"][:].unsqueeze(1).broadcast_to([128, fq, TTW])

                def mksub(dsta=dsta, x1a=x1a, mba=mba, a=a, tt=tt):
                    p.step("dve", lambda e, dsta=dsta, x1a=x1a, mba=mba:
                           e.tensor_tensor(dsta, x1a, mba, ALU.subtract),
                           reads=x1cb[a * fq:(a + 1) * fq] + [B["MBF"]],
                           writes=[lnbb[tt]])

                def mkmul(dsta=dsta, rba=rba, tt=tt):
                    p.step("dve", lambda e, dsta=dsta, rba=rba:
                           e.tensor_tensor(dsta, dsta, rba, ALU.mult),
                           reads=[B["RBF"], lnbb[tt]], writes=[lnbb[tt]])
                ln2p += [mksub, mkmul]
            if defer_ln2:
                return ln2p
            for fn in ln2p:
                fn()
        emit_gate(0)
        emit_rows_p(0)
        if c.TT > 1:
            emit_gate(1)
            pieces = emit_rows_p(1, defer=True)
            ln2p = emit_out(0, inject=pieces, defer_ln2=True)
            emit_out(1, inject=ln2p, inj_start=0, inj_step=1)
        else:
            emit_out(0)

        dma("gp", ap("LGA"), lga, "LGA", writes=[B["LGA"]],
            aliases=woverlap["LGA"])
        dma("gp", ap("LUA"), lua, "LUA", writes=[B["LUA"]],
            aliases=woverlap["LUA"])
        ngr = TTW // 128
        abank = (("psA", "psB"), ("psC", "psD"))

        def prep_router(tt):
            rpairs = [(T["WRB"][:, fc, :], T["LNB"][:, fc, tt, :])
                      for fc in range(c.FC)]
            mm_group(ps["psC"][0:8, :], rpairs, B["psC"], [B["WRB"], lnbb[tt]])
            p.step("act", lambda e, tt=tt: e.activation(
                T["SC"][:, tt * TTW:(tt + 1) * TTW], ps["psC"][0:8, :], AF.Sigmoid),
                reads=[B["psC"]], writes=[B["SC"]])

        def prep_top2(tt):
            def scfn(e, tt=tt):
                last = None
                for gidx in range(ngr):
                    last = e.transpose(
                        ps["psT"][:, 0, gidx * 8:(gidx + 1) * 8],
                        T["SC"][0:8, tt * TTW + gidx * 128: tt * TTW + (gidx + 1) * 128],
                        T["IDNB"][0:8, 0:8])
                return last
            p.step("pe", scfn, reads=[B["SC"], B["IDNB"]], writes=[B["psT"]])
            p.step("dve", lambda e, tt=tt: e.tensor_copy(
                T["SCT"][:, tt], ps["psT"][:, 0, 0:ngr * 8].rearrange(
                    "p (g e) -> p g e", e=8)),
                reads=[B["psT"]], writes=[B["SCT"]])
            bb = T["BIAS"][:].unsqueeze(1).broadcast_to([128, ngr, 8])
            p.step("dve", lambda e, tt=tt, bb=bb: e.tensor_tensor(
                T["BSD"][:], T["SCT"][:, tt], bb, ALU.add),
                reads=[B["SCT"], B["BIAS"]], writes=[B["BSD"]])
            p.step("dve", lambda e: e.reduce_max(T["M1"][:], T["BSD"][:], axis=AX.X),
                   reads=[B["BSD"]], writes=[B["M1"]])
            m1b = T["M1"][:].unsqueeze(2).broadcast_to([128, ngr, 8])
            p.step("dve", lambda e, m1b=m1b: e.tensor_tensor(
                T["TOP"][:], T["BSD"][:], m1b, ALU.is_ge),
                reads=[B["BSD"], B["M1"]], writes=[B["TOP"]])
            p.step("dve", lambda e: e.scalar_tensor_tensor(
                T["BSD"][:], T["TOP"][:], -1e9, T["BSD"][:], ALU.mult, ALU.add),
                reads=[B["TOP"], B["BSD"]], writes=[B["BSD"]])
            p.step("dve", lambda e: e.reduce_max(T["M1"][:], T["BSD"][:], axis=AX.X),
                   reads=[B["BSD"]], writes=[B["M1"]])
            p.step("dve", lambda e, m1b=m1b: e.tensor_tensor(
                T["WTM"][:], T["BSD"][:], m1b, ALU.is_ge),
                reads=[B["BSD"], B["M1"]], writes=[B["WTM"]])
            p.step("dve", lambda e: e.tensor_add(
                T["TOP"][:], T["TOP"][:], T["WTM"][:]),
                reads=[B["WTM"], B["TOP"]], writes=[B["TOP"]])
            p.step("dve", lambda e, tt=tt: e.tensor_tensor(
                T["WTM"][:], T["TOP"][:], T["SCT"][:, tt], ALU.mult),
                reads=[B["TOP"], B["SCT"]], writes=[B["WTM"]])
            p.step("dve", lambda e: e.reduce_sum(T["M1"][:], T["WTM"][:], axis=AX.X),
                   reads=[B["WTM"]], writes=[B["M1"]])
            p.step("dve", lambda e: e.tensor_scalar_add(
                T["M1"][:], T["M1"][:], TOPEPS),
                reads=[B["M1"]], writes=[B["M1"]])
            p.step("dve", lambda e: e.reciprocal(T["M1"][:], T["M1"][:]),
                   reads=[B["M1"]], writes=[B["M1"]])
            p.step("dve", lambda e, m1b=m1b, tt=tt: e.tensor_tensor(
                T["WTM"][:], T["WTM"][:], m1b, ALU.mult),
                reads=[B["M1"], B["WTM"]], writes=[B["WTM"]])
        def prep_wet(tt):
            def wtfn(e):
                last = None
                for gidx in range(ngr):
                    last = e.transpose(
                        ps["psQK"][0:8, gidx * 128:(gidx + 1) * 128],
                        T["WTM"][:, gidx, :], T["IDNF"][:])
                return last
            p.step("pe", wtfn, reads=[B["WTM"], B["IDNF"]], writes=[B["psQK"]])
            p.step("dve", lambda e: e.tensor_copy(T["WETB"][:], ps["psQK"][0:8, :]),
                   reads=[B["psQK"]], writes=[B["WETB"]])
            p.step("pe", lambda e: e.matmul(
                ps["psQK"][0:64, :], T["E8"][:], T["WETB"][:], start=True, stop=True),
                reads=[B["E8"], B["WETB"]], writes=[B["psQK"]])
            p.step("dve", lambda e, tt=tt: e.tensor_copy(
                T["WREP"][:, tt, :], ps["psQK"][0:64, :]),
                reads=[B["psQK"]], writes=[B["WREP"]])

        def prep_a(tt):
            ga, ua = abank[tt]
            gpairs = [(T["LGA"][:, fc, :], T["LNB"][:, fc, tt, :])
                      for fc in range(c.FC)]
            upairs = [(T["LUA"][:, fc, :], T["LNB"][:, fc, tt, :])
                      for fc in range(c.FC)]
            mm_group(ps[ga][0:72, :], gpairs, B[ga], [B["LGA"], lnbb[tt]])
            mm_group(ps[ua][0:72, :], upairs, B[ua], [B["LUA"], lnbb[tt]])

        def prep_m(tt):
            ga, ua = abank[tt]
            p.step("dve", lambda e, tt=tt, ga=ga: e.tensor_tensor(
                T["MG"][0:64, tt, :], ps[ga][0:64, :], T["WREP"][:, tt, :],
                ALU.mult), reads=[B[ga], B["WREP"]], writes=[mgb[tt]])
            p.step("dve", lambda e, tt=tt, ga=ga: e.tensor_scalar_mul(
                T["MG"][64:72, tt, :], ps[ga][64:72, :], 0.125),
                reads=[B[ga]], writes=[mgb[tt]])
            p.step("dve", lambda e, tt=tt, ua=ua: e.tensor_tensor(
                T["MU"][0:64, tt, :], ps[ua][0:64, :], T["WREP"][:, tt, :],
                ALU.mult), reads=[B[ua], B["WREP"]], writes=[mub[tt]])
            p.step("dve", lambda e, tt=tt, ua=ua: e.tensor_scalar_mul(
                T["MU"][64:72, tt, :], ps[ua][64:72, :], 0.125),
                reads=[B[ua]], writes=[mub[tt]])
        prep_router(0)
        prep_a(0)
        prep_top2(0)
        if c.TT > 1:
            prep_router(1)
            prep_a(1)
        prep_wet(0)
        prep_m(0)
        if c.TT > 1:
            prep_top2(1)
        p.step("dve", lambda e: e.memset(T["FEN"][:], 0.0),
               writes=[B["HID"], B["FEN"]] + tenant_bufs + qkf_bufs
               + [bb for tl in attcb for bb in tl])
        dma("gp", ap("LDA"), lda, "LDA", writes=[B["LDA"]],
            aliases=woverlap["LDA"])
        for dc in range(c.DC):
            wgb = ("WGT0", "WGT1")[dc % 2]
            wub = ("WUP0", "WUP1")[dc % 2]
            wload(ci, wgb, wgate[dc], wgates[dc], ("wgate", dc))
            wload(ci, wub, wup[dc], wups[dc], ("wup", dc))
            for tt in range(c.TT):
                if dc == 0 and tt == 1:
                    prep_wet(1)
                    prep_m(1)
                gp_ = ("psA", "psC")[tt]
                up_ = ("psB", "psD")[tt]
                gpairs = [(T[wgb][:, fc, :], T["LNB"][:, fc, tt, :])
                          for fc in range(c.FC)]
                gpairs.append((T[wgb][0:72, c.FC, :], T["MG"][:, tt, :]))
                upairs = [(T[wub][:, fc, :], T["LNB"][:, fc, tt, :])
                          for fc in range(c.FC)]
                upairs.append((T[wub][0:72, c.FC, :], T["MU"][:, tt, :]))
                mm_group(ps[gp_][:], gpairs, B[gp_], [B[wgb], lnbb[tt], mgb[tt]])
                mm_group(ps[up_][:], upairs, B[up_], [B[wub], lnbb[tt], mub[tt]])
                hs = ("HS0", "HS1")[tt]
                p.step("act", lambda e, hs=hs, gp_=gp_: e.activation(
                    ap(hs), ps[gp_][:], AF.Silu), reads=[B[gp_]], writes=[B[hs]])
                p.step("dve", lambda e, hs=hs, up_=up_, dc=dc, tt=tt: e.tensor_tensor(
                    T["HID"][:, dc, tt, :], ps[up_][:], ap(hs), ALU.mult),
                    reads=[B[up_], B[hs]], writes=[B["HID"]])

        for tt in range(c.TT):
            dpairs = [(T["LDA"][:, fc, :], T["HID"][:, fc, tt, :])
                      for fc in range(c.DC)]
            mm_group(ps["psA"][0:72, :], dpairs, B["psA"], [B["LDA"], B["HID"]])
            p.step("dve", lambda e, tt=tt: e.tensor_tensor(
                T["MD"][0:64, tt, :], ps["psA"][0:64, :], T["WREP"][:, tt, :],
                ALU.mult), reads=[B["psA"], B["WREP"]], writes=[B["MD"]])
            p.step("dve", lambda e, tt=tt: e.tensor_scalar_mul(
                T["MD"][64:72, tt, :], ps["psA"][64:72, :], 0.125),
                reads=[B["psA"]], writes=[B["MD"]])
        nxt = ci + 1 < c.NC
        ride = []
        if nxt and c.TT > 1:
            for att in range(c.TT):
                for fc in range(c.FC):
                    ride.append(lambda att=att, fc=fc:
                                ln1_stat_unit(att, fc, ci + 1))
                ride.append(lambda att=att: ln1_rows(att, ci + 1))
                for a in range(4):
                    ride.append(lambda att=att, a=a:
                                ln1_apply_piece(att, a, ci + 1))
        rpos = [0]
        for oc in range(c.FC):
            wdb = ("WDN0", "WDN1")[oc % 2]
            wload(ci, wdb, wdown[oc], wdowns[oc], ("wdown", oc))
            for tt in range(c.TT):
                stg = ("STG2", "STG3")[tt]
                dma("sp", ap(stg), x1s[ci, tt, :, oc, :], stg,
                    reads=[x1sb[(tt, oc)]], writes=[B[stg]])
                pairs = [(T[wdb][:, fc, :], T["HID"][:, fc, tt, :])
                         for fc in range(c.DC)]
                pairs.append((T[wdb][0:72, c.DC, :], T["MD"][:, tt, :]))
                pb = ("psA", "psB", "psD")[(2 * oc + tt) % 3]
                mm_group(ps[pb][:], pairs, B[pb], [B[wdb], B["HID"], B["MD"]])
                p.step("dve", lambda e, stg=stg, pb=pb: e.tensor_tensor(
                    ap(stg), ap(stg), ps[pb][:], ALU.add),
                    reads=[B[pb], B[stg]], writes=[B[stg]])
                dma("sp", out[ci, tt, :, oc, :], ap(stg), stg, reads=[B[stg]])
            if nxt and oc == c.FC - 4:
                wload(ci + 1, "WQKB1", wqk[0], wqks[0], ("wqk", 0), q="sp")
                preloaded.add(("wqk", 0))
            if nxt:
                if c.TT > 1:
                    take = ride[rpos[0]:rpos[0] + 4]
                    rpos[0] += len(take)
                    for it in take:
                        it()
                else:
                    ln1_stat_unit(0, oc, ci + 1)
                    if oc == c.FC - 1:
                        ln1_apply(0, ci + 1)

    p.emit()
    nc.finalize()
    return nc


def kernel(**inputs):
    x = np.asarray(inputs["L0_attn_hidden_states"])
    Bsz, NT, D = x.shape
    H = int(np.asarray(inputs["L0_attn_slope_rate"]).shape[0])
    DFF = int(np.asarray(inputs["L0_ffn_W_gate"]).shape[1])
    cfg = Cfg(D=D, H=H, NT=NT, DFF=DFF, chunk=1024)
    shared, xs = prep_inputs(cfg, inputs)
    nc = build_nc(cfg)
    in_maps = [dict(shared, xt=xs[b]) for b in range(Bsz)]
    res = run_bass_kernel_spmd(nc, in_maps, core_ids=list(range(Bsz)))
    outs = [r["out"] for r in res.results]
    return unshard_out(cfg, outs)


# version 43
# speedup vs baseline: 1.0702x; 1.0702x over previous
import numpy as np
import ml_dtypes
import concourse.bass as bass
import concourse.mybir as mybir
from concourse.bass_utils import run_bass_kernel_spmd

F32 = mybir.dt.float32
BF16 = mybir.dt.bfloat16
AF = mybir.ActivationFunctionType
ALU = mybir.AluOpType
AX = mybir.AxisListType
BF = ml_dtypes.bfloat16

LNEPS = 1e-5
RMSEPS = 1e-6
TOPEPS = 1e-8
BLK = 256
TTW = 512


class Cfg:
    def __init__(self, D, H, NT, DFF, chunk=1024):
        self.D = D
        self.H = H
        self.DH = 128
        self.NT = NT
        self.DFF = DFF
        self.E = 8
        self.R = 8
        self.LR = 72
        self.chunk = min(chunk, NT)
        assert self.chunk % TTW == 0 and NT % self.chunk == 0
        self.NC = NT // self.chunk
        self.TT = self.chunk // TTW
        self.NB = self.chunk // BLK
        self.NG = self.chunk // 128
        self.FC = D // 128
        self.DC = DFF // 128
        assert D % 128 == 0 and DFF % 128 == 0 and H % 4 == 0
        assert D == H * 128


def _tile_w(wt, pw):
    F, O = wt.shape
    a = wt.reshape(F // 128, 128, O // pw, pw).transpose(2, 1, 0, 3)
    return np.ascontiguousarray(a, dtype=np.float32)


def _pack_lora(wt, bmat, pw=128):
    F, O = wt.shape
    a = _tile_w(wt, pw)
    bpad = np.zeros((O // pw, 128, 1, pw), np.float32)
    b = bmat.reshape(72, O // pw, pw).transpose(1, 0, 2)
    bpad[:, :72, 0, :] = b
    return np.ascontiguousarray(np.concatenate([a, bpad], axis=2))


def prep_inputs(cfg, ins):
    c = cfg
    f32 = np.float32
    wqkv = np.asarray(ins["L0_attn_w_qkv"], f32)
    wqkvT = np.ascontiguousarray(wqkv.T)
    D, H = c.D, c.H
    qk = np.empty((H, 128, c.FC, 256), f32)
    for h in range(H):
        qh = wqkvT[:, h * 128:(h + 1) * 128]
        kh = wqkvT[:, D + h * 128: D + (h + 1) * 128]
        pan = np.concatenate([qh, kh], axis=1)
        qk[h] = pan.reshape(c.FC, 128, 256).transpose(1, 0, 2)
    wv = np.empty((H // 4, 128, c.FC, 512), f32)
    vT = wqkvT[:, 2 * D:3 * D]
    for g in range(H // 4):
        pan = vT[:, g * 512:(g + 1) * 512]
        wv[g] = pan.reshape(c.FC, 128, 512).transpose(1, 0, 2)

    wg = _tile_w(np.ascontiguousarray(np.asarray(ins["L0_attn_w_gate"], f32).T), 256)
    wo = _tile_w(np.ascontiguousarray(np.asarray(ins["L0_attn_w_out"], f32).T), 256)

    def lstack(A, sA):
        A = np.asarray(A, f32); sA = np.asarray(sA, f32)
        a = A.transpose(1, 0, 2).reshape(A.shape[1], 64)
        st = np.concatenate([a, sA], axis=1)
        return np.ascontiguousarray(
            st.reshape(st.shape[0] // 128, 128, 72).transpose(1, 0, 2))

    def bstack(B, sB):
        B = np.asarray(B, f32).reshape(64, -1)
        return np.concatenate([B, np.asarray(sB, f32)], axis=0)

    wgate = _pack_lora(np.asarray(ins["L0_ffn_W_gate"], f32),
                       bstack(ins["L0_ffn_B_gate"], ins["L0_ffn_sB_gate"]))
    wup = _pack_lora(np.asarray(ins["L0_ffn_W_up"], f32),
                     bstack(ins["L0_ffn_B_up"], ins["L0_ffn_sB_up"]))
    wdown = _pack_lora(np.asarray(ins["L0_ffn_W_down"], f32),
                       bstack(ins["L0_ffn_B_down"], ins["L0_ffn_sB_down"]))

    wr = np.ascontiguousarray(
        np.asarray(ins["L0_ffn_W_router"], f32).reshape(c.FC, 128, 8).transpose(1, 0, 2))

    nw = np.asarray(ins["L0_attn_norm_w"], f32).reshape(c.FC, 128).T

    i0 = np.arange(BLK, dtype=f32)
    posm = (i0[None, :] - i0[:, None])
    posm = np.where(posm >= 0, posm, 1e9).astype(f32)
    posm = np.ascontiguousarray(posm.reshape(2, 128, BLK).transpose(1, 0, 2))
    rowi = np.broadcast_to(i0[None, :] + 1.0, (128, BLK)).astype(f32)
    rowkd = np.empty((128, 2), f32)
    for par in range(2):
        rowkd[:, par] = 255.0 - (par * 128 + np.arange(128, dtype=f32))
    e8 = np.zeros((8, 64), f32)
    for e in range(8):
        e8[e, e * 8:(e + 1) * 8] = 0.125
    shared = {
        "wqk": qk, "wv": wv, "wg": wg, "wo": wo,
        "wgate": wgate, "wup": wup, "wdown": wdown, "wr": wr,
        "lga": lstack(ins["L0_ffn_A_gate"], ins["L0_ffn_sA_gate"]),
        "lua": lstack(ins["L0_ffn_A_up"], ins["L0_ffn_sA_up"]),
        "lda": lstack(ins["L0_ffn_A_down"], ins["L0_ffn_sA_down"]),
        "slope": np.asarray(ins["L0_attn_slope_rate"], f32).reshape(1, H),
        "ebias": np.asarray(ins["L0_ffn_expert_bias"], f32).reshape(1, 8),
        "normw": np.ascontiguousarray(nw),
        "idnb": np.eye(128, dtype=f32).astype(BF),
        "idnf": np.eye(128, dtype=f32),
        "onesb": np.ones((128, 128), f32).astype(BF),
        "onesf": np.ones((1, 128), f32),
        "rowi": rowi, "posm": posm, "rowkd": rowkd,
        "e8": e8.astype(BF),
    }
    x = np.asarray(ins["L0_attn_hidden_states"], f32)
    xs = []
    for b in range(x.shape[0]):
        xt = np.ascontiguousarray(x[b].T)
        xt = xt.reshape(c.FC, 128, c.NC, c.TT, TTW).transpose(2, 3, 1, 0, 4)
        xs.append(np.ascontiguousarray(xt))
    return shared, xs


def unshard_out(cfg, outs):
    c = cfg
    res = []
    for o in outs:
        a = o.transpose(3, 2, 0, 1, 4).reshape(c.D, c.NT)
        res.append(a.T)
    return np.ascontiguousarray(np.stack(res)).astype(np.float32)


class Buf:
    __slots__ = ("w", "r", "name")

    def __init__(self, name=""):
        self.name = name
        self.w = None
        self.r = []


class Prog:
    ENGS = ("pe", "act", "dve", "sp", "gp")

    def __init__(self, nc):
        self.nc = nc
        self.stream = {e: [] for e in self.ENGS}
        self.semh = {s: nc.alloc_semaphore("S" + s)
                     for s in ("pe", "act", "dve", "gpc")}
        self.cnt = {s: 0 for s in self.semh}
        self.hw = {}
        self.esem = {"pe": "pe", "act": "act", "dve": "dve", "gp": "gpc"}

    def dsem(self, name):
        if name not in self.semh:
            self.semh[name] = self.nc.alloc_semaphore("D" + name)
            self.cnt[name] = 0
        return name

    def step(self, eng, fn, reads=(), writes=(), dma=None, ndma=1, aliases=()):
        toks = []
        for b in reads:
            if b.w is not None:
                toks.append(b.w)
        for b in writes:
            if b.w is not None:
                toks.append(b.w)
            toks.extend(b.r)
        for b in aliases:
            if b.w is not None:
                toks.append(b.w)
            toks.extend(b.r)
        need = {}
        for s, v in toks:
            if need.get(s, -1) < v:
                need[s] = v
        waits = []
        for s, v in need.items():
            key = (eng, s)
            if self.hw.get(key, 0) >= v:
                continue
            self.hw[key] = v
            waits.append((s, v))
        if dma is not None:
            sname = self.dsem(dma)
            amt = 16
            self.cnt[sname] += 16 * ndma
        else:
            sname = self.esem[eng]
            amt = 1
            self.cnt[sname] += 1
        tok = (sname, self.cnt[sname])
        self.stream[eng].append((waits, fn, sname, amt))
        for b in reads:
            b.r.append(tok)
        for b in writes:
            b.w = tok
            b.r = []
        for b in aliases:
            b.r.append(tok)
        return tok

    def emit(self):
        nc = self.nc
        semh = self.semh
        stream = self.stream
        finals = dict(self.cnt)

        with nc.Block() as block:
            def make(key):
                def body(e):
                    for waits, fn, sname, amt in stream[key]:
                        for s, v in waits:
                            e.wait_ge(semh[s], v)
                        last = fn(e)
                        if isinstance(last, list):
                            for ins in last:
                                ins.then_inc(semh[sname], amt)
                        else:
                            assert last is not None, f"step on {key} -> None"
                            last.then_inc(semh[sname], amt)
                    if key == "sp":
                        for s, v in finals.items():
                            if v > self.hw.get(("sp", s), 0):
                                e.wait_ge(semh[s], v)
                return body
            block.tensor(make("pe"))
            block.scalar(make("act"))
            block.vector(make("dve"))
            block.sync(make("sp"))
            block.gpsimd(make("gp"))


def build_nc(cfg):
    c = cfg
    nc = bass.Bass("TRN2", target_bir_lowering=False)
    p = Prog(nc)

    def din(name, shape, dt=F32):
        return nc.dram_tensor(name, list(shape), dt, kind="ExternalInput").ap()

    xt = din("xt", (c.NC, c.TT, 128, c.FC, TTW))
    wqk = din("wqk", (c.H, 128, c.FC, 256))
    wv = din("wv", (c.H // 4, 128, c.FC, 512))
    wg = din("wg", (c.FC // 2, 128, c.FC, 256))
    wo = din("wo", (c.FC // 2, 128, c.FC, 256))
    wgate = din("wgate", (c.DC, 128, c.FC + 1, 128))
    wup = din("wup", (c.DC, 128, c.FC + 1, 128))
    wdown = din("wdown", (c.FC, 128, c.DC + 1, 128))
    wr = din("wr", (128, c.FC, 8))
    lga = din("lga", (128, c.FC, 72))
    lua = din("lua", (128, c.FC, 72))
    lda = din("lda", (128, c.DC, 72))
    slope = din("slope", (1, c.H))
    ebias = din("ebias", (1, 8))
    normw = din("normw", (128, c.FC))
    idnb = din("idnb", (128, 128), BF16)
    idnf = din("idnf", (128, 128))
    onesb = din("onesb", (128, 128), BF16)
    onesf = din("onesf", (1, 128))
    rowi = din("rowi", (128, BLK))
    posm = din("posm", (128, 2, BLK))
    rowkd = din("rowkd", (128, 2))
    e8 = din("e8", (8, 64), BF16)

    out = nc.dram_tensor("out", [c.NC, c.TT, 128, c.FC, TTW], F32,
                         kind="ExternalOutput").ap()
    tabs = nc.dram_tensor("tabs", [c.H, 128, 3, BLK], BF16, kind="Internal").ap()
    wqks = nc.dram_tensor("wqks", [c.H, 128, c.FC, 256], BF16, kind="Internal").ap()
    wvs = nc.dram_tensor("wvs", [c.H // 4, 128, c.FC, 512], BF16, kind="Internal").ap()
    wgs = nc.dram_tensor("wgs", [c.FC // 2, 128, c.FC, 256], BF16, kind="Internal").ap()
    wos = nc.dram_tensor("wos", [c.FC // 2, 128, c.FC, 256], BF16, kind="Internal").ap()
    wgates = nc.dram_tensor("wgates", [c.DC, 128, c.FC + 1, 128], BF16, kind="Internal").ap()
    wups = nc.dram_tensor("wups", [c.DC, 128, c.FC + 1, 128], BF16, kind="Internal").ap()
    wdowns = nc.dram_tensor("wdowns", [c.FC, 128, c.DC + 1, 128], BF16, kind="Internal").ap()
    x1s = nc.dram_tensor("x1s", [c.NC, c.TT, 128, c.FC, TTW], F32,
                         kind="Internal").ap()

    sb = {}
    bufs = {}

    def S(name, shape, dt=F32, offset=None):
        if offset is None:
            h = nc.alloc_sbuf_tensor(name, list(shape), dt)
        else:
            h = nc.alloc_sbuf_tensor_at(name, list(shape), dt, offset=offset)
        sb[name] = h
        bufs[name] = Buf(name)
        return h

    S("QF0", (128, c.chunk), BF16)
    S("QD0", (128, c.chunk), BF16)
    S("KF0", (128, c.chunk), BF16)
    S("KD0", (128, c.NG, 128), BF16)
    S("QKD", (128, 2, BLK), BF16)
    S("QKD2", (128, 2, BLK), BF16)
    S("QKD3", (128, 2, BLK), BF16)
    S("KVB", (128, c.H, 128), BF16)
    S("DT0", (128, 2, BLK), BF16)
    S("DT1", (128, 2, BLK), BF16)
    S("QDC0", (128, BLK), BF16)
    S("QDC1", (128, BLK), BF16)
    S("KDC", (128, c.H, 2))
    S("SB", (128, c.H))
    S("BD", (128, c.H))
    S("BIAS", (128, 8))
    S("NW", (128, c.FC))
    S("ONESB", (128, 128), BF16)
    S("IDNB", (128, 128), BF16)
    S("IDNF", (128, 128))
    S("ONESF", (1, 128))
    S("ROWKD", (128, 2))
    S("ROWA", (128, 4, TTW))
    S("MBF", (128, TTW), BF16)
    S("RBF", (128, TTW), BF16)
    S("XSQ0", (128, TTW), BF16)
    S("XSQ1", (128, TTW), BF16)
    S("XSQ2", (128, TTW), BF16)
    S("E8", (8, 64), BF16)
    S("SLOPE", (1, c.H))
    S("EPSC", (128, 2))
    S("FEN", (128, 1))
    S("EBIAS", (1, 8))
    S("WRB", (128, c.FC, 8), BF16)
    S("WREP", (64, c.TT, TTW), BF16)
    S("MG", (72, c.TT, TTW), BF16)
    S("MU", (72, c.TT, TTW), BF16)
    S("MD", (72, c.TT, TTW), BF16)
    S("SC", (8, c.chunk), BF16)
    S("SCT", (128, c.TT, TTW // 128, 8))
    S("BSD", (128, TTW // 128, 8))
    S("WTM", (128, TTW // 128, 8))
    S("TOP", (128, TTW // 128, 8))
    S("M1", (128, TTW // 128))
    S("WETB", (8, TTW), BF16)
    S("HS0", (128, TTW), BF16)
    S("HS1", (128, TTW), BF16)
    S("STG0", (128, TTW))
    S("STG1", (128, TTW))
    S("STG2", (128, TTW))
    S("STG3", (128, TTW))
    S("LNB", (128, c.FC, c.TT, TTW), BF16)
    wten = [
        ("WQKB", (128, c.FC, 256), 0),
        ("WQKB1", (128, c.FC, 256), 3 * c.FC * 512),
        ("WVB0", (128, c.FC, 512), c.FC * 512),
        ("WB0", (128, c.FC, 256), 0),
        ("WB1", (128, c.FC, 256), c.FC * 512),
        ("WB2", (128, c.FC, 256), 2 * c.FC * 512),
        ("WB3", (128, c.FC, 256), 3 * c.FC * 512),
        ("WGT0", (128, c.FC + 1, 128), 0),
        ("WGT1", (128, c.FC + 1, 128), (c.FC + 1) * 256),
        ("WUP0", (128, c.FC + 1, 128), 2 * (c.FC + 1) * 256),
        ("WUP1", (128, c.FC + 1, 128), 3 * (c.FC + 1) * 256),
        ("WDN0", (128, c.DC + 1, 128), 0),
        ("WDN1", (128, c.DC + 1, 128), (c.DC + 1) * 256),
        ("LGA", (128, c.FC, 72), 0),
        ("LUA", (128, c.FC, 72), c.FC * 144),
        ("LDA", (128, c.DC, 72), 4 * (c.FC + 1) * 256),
    ]
    wa_bytes = max(o + int(np.prod(sh[1:])) * 2 for _, sh, o in wten)
    wa = nc.alloc_sbuf_tensor("WARENA", [128, wa_bytes], mybir.dt.uint8)
    wabase = nc.lookup_mloc(wa).addr
    wrange = {}
    for nm, sh, o in wten:
        sb[nm] = nc.alloc_sbuf_tensor_at(nm, list(sh), BF16, offset=wabase + o)
        bufs[nm] = Buf(nm)
        wrange[nm] = (o, o + int(np.prod(sh[1:])) * 2)
    woverlap = {}
    for nm, (a0, a1) in wrange.items():
        woverlap[nm] = [bufs[m] for m, (b0, b1) in wrange.items()
                        if m != nm and a0 < b1 and b0 < a1]

    hid_bytes = c.DC * c.chunk * 2
    tenants = [
        ("GT", (128, c.FC, TTW), BF16, None),
        ("ATT", (128, c.H, c.TT, TTW), BF16, None),
        ("X1B", (128, c.FC, TTW), BF16, None),
        ("VTM", (128, c.NG, 512), BF16, None),
        ("QF1", (128, c.chunk), BF16, None),
        ("QD1", (128, c.chunk), BF16, None),
        ("KF1", (128, c.chunk), BF16, None),
        ("KD1", (128, c.NG, 128), BF16, None),
        ("ROWI", (128, BLK), F32, None),
        ("POSM", (128, 2, BLK), F32, None),
    ]
    ten_bytes = 0
    for name, shape, dt, alias in tenants:
        if alias is None:
            n = int(np.prod(shape[1:])) * mybir.dt.size(dt)
            ten_bytes += (n + 31) // 32 * 32
    region_bytes = max(hid_bytes, ten_bytes)
    reg = nc.alloc_sbuf_tensor("OVL", [128, region_bytes], mybir.dt.uint8)
    rbase = nc.lookup_mloc(reg).addr
    cur = rbase
    offs = {}
    for name, shape, dt, alias in tenants:
        if alias is not None:
            offs[name] = offs[alias]
        else:
            offs[name] = cur
            n = int(np.prod(shape[1:])) * mybir.dt.size(dt)
            cur += (n + 31) // 32 * 32
    for name, shape, dt, alias in tenants:
        h = nc.alloc_sbuf_tensor_at(name, list(shape), dt, offset=offs[name])
        sb[name] = h
        bufs[name] = bufs[alias] if alias is not None else Buf(name)
    S("HID", (128, c.DC, c.TT, TTW), BF16, offset=rbase)
    tenant_bufs = [bufs[n] for n, _, _, a in tenants if a is None]

    ps = {}
    for name in ("psA", "psB", "psC", "psD"):
        ps[name] = nc.alloc_psum_tensor(name, [128, TTW], F32)
        bufs[name] = Buf(name)
    ps["psQK"] = nc.alloc_psum_tensor("psQK", [128, TTW], F32)
    bufs["psQK"] = Buf("psQK")
    ps["psAO"] = nc.alloc_psum_tensor("psAO", [128, BLK], F32)
    bufs["psAO"] = Buf("psAO")
    ps["psT"] = nc.alloc_psum_tensor("psT", [128, c.NG, 128], BF16)
    bufs["psT"] = Buf("psT")
    ps["psKV"] = nc.alloc_psum_tensor("psKV", [128, 128], F32)
    bufs["psKV"] = Buf("psKV")

    B = bufs
    T = sb

    def ap(name):
        return T[name][:]

    kvb_b = [Buf(f"kvb{h}") for h in range(c.H)]
    lnbb = [Buf(f"lnb{t}") for t in range(c.TT)]
    attcb = [[Buf(f"att{t}_{o}") for o in range(c.FC)] for t in range(c.TT)]
    x1cb = [Buf(f"x1c{o}") for o in range(c.FC)]
    lnxb = [[Buf(f"lnx{t}_{s}") for s in range(4)] for t in range(c.TT)]
    mgb = [Buf(f"mg{t}") for t in range(c.TT)]
    mub = [Buf(f"mu{t}") for t in range(c.TT)]
    qftb = [[Buf(f"qf{s}_{t}") for t in range(c.TT)] for s in range(2)]
    kftb = [[Buf(f"kf{s}_{t}") for t in range(c.TT)] for s in range(2)]
    qkf_bufs = [b_ for sl in qftb + kftb for b_ in sl]

    def dma(eng, outap, inap, sem, reads=(), writes=(), aliases=()):
        return p.step(eng, lambda e, o=outap, i=inap: e.dma_start(out=o, in_=i),
                      reads=reads, writes=writes, dma=sem, aliases=aliases)

    cinit0 = [("ONESB", onesb), ("ONESF", onesf), ("SLOPE", slope),
              ("IDNB", idnb)]
    cinit = [("IDNF", idnf), ("ROWI", rowi), ("POSM", posm),
             ("ROWKD", rowkd), ("E8", e8), ("EBIAS", ebias), ("NW", normw)]

    def cinit0_fn(e):
        return [e.dma_start(out=ap(nm), in_=src) for nm, src in cinit0]
    p.step("sp", cinit0_fn, writes=[B[nm] for nm, _ in cinit0],
           dma="cinit0", ndma=len(cinit0))

    def cinit_fn(e):
        return [e.dma_start(out=ap(nm), in_=src) for nm, src in cinit]
    p.step("sp", cinit_fn, writes=[B[nm] for nm, _ in cinit],
           dma="cinit", ndma=len(cinit))

    ginit = [("WRB", wr)]

    def ginit_fn(e):
        return [e.dma_start(out=ap(nm), in_=src) for nm, src in ginit]
    p.step("gp", ginit_fn, writes=[B[nm] for nm, _ in ginit],
           dma="ginit", ndma=len(ginit))

    p.step("pe", lambda e: e.matmul(ps["psA"][:, 0:c.H], T["ONESF"][:],
                                    T["SLOPE"][:], start=True, stop=True),
           reads=[B["ONESF"], B["SLOPE"]], writes=[B["psA"]])
    p.step("dve", lambda e: e.tensor_copy(ap("SB"), ps["psA"][:, 0:c.H]),
           reads=[B["psA"]], writes=[B["SB"]])
    p.step("pe", lambda e: e.matmul(ps["psB"][:, 0:8], T["ONESF"][:],
                                    T["EBIAS"][:], start=True, stop=True),
           reads=[B["ONESF"], B["EBIAS"]], writes=[B["psB"]])
    p.step("dve", lambda e: e.tensor_copy(ap("BIAS"), ps["psB"][:, 0:8]),
           reads=[B["psB"]], writes=[B["BIAS"]])
    p.step("act", lambda e: e.activation(ap("BD"), ap("SB"), AF.Exp,
                                         scale=-float(BLK)),
           reads=[B["SB"]], writes=[B["BD"]])
    rowa = T["ROWA"]
    for h in range(c.H):
        p.step("dve", lambda e, h=h: e.tensor_scalar_mul(
            rowa[:, 0, 0:2], T["ROWKD"][:], T["SB"][:, h:h + 1]),
            reads=[B["ROWKD"], B["SB"]], writes=[B["ROWA"]])
        p.step("act", lambda e, h=h: e.activation(
            T["KDC"][:, h, :], rowa[:, 0, 0:2], AF.Exp, scale=-1.0),
            reads=[B["ROWA"]], writes=[B["KDC"]])
    tabs_b = [Buf(f"tab{h}") for h in range(c.H)]

    def emit_table_init():
        s0v = T["STG0"][:].rearrange("p (a b) -> p a b", b=BLK)
        for h in range(c.H):
            stg = ("DT0", "DT1")[h % 2]
            stgq = ("QDC0", "QDC1")[h % 2]
            sB_, sQ = B[stg], B[stgq]
            p.step("dve", lambda e, h=h: e.tensor_scalar_mul(
                s0v, T["POSM"][:], T["SB"][:, h:h + 1]),
                reads=[B["POSM"], B["SB"]], writes=[B["STG0"]])
            p.step("act", lambda e, stg=stg: e.activation(
                T[stg][:], s0v, AF.Exp, scale=-1.0),
                reads=[B["STG0"]], writes=[sB_])
            p.step("dve", lambda e, h=h: e.tensor_scalar_mul(
                T["STG1"][:, 0:BLK], T["ROWI"][:], T["SB"][:, h:h + 1]),
                reads=[B["ROWI"], B["SB"]], writes=[B["STG1"]])
            p.step("act", lambda e, stgq=stgq: e.activation(
                T[stgq][:], T["STG1"][:, 0:BLK], AF.Exp, scale=-1.0),
                reads=[B["STG1"]], writes=[sQ])
            p.step("sp", lambda e, h=h, stg=stg, stgq=stgq: [
                e.dma_start(out=tabs[h, :, 0:2, :], in_=T[stg][:]),
                e.dma_start(out=tabs[h, :, 2, :], in_=T[stgq][:])],
                reads=[sB_, sQ], writes=[tabs_b[h]], dma=f"tabst{h % 2}", ndma=2)
    p.step("dve", lambda e: e.memset(ap("KVB"), 0.0), writes=[B["KVB"]] + kvb_b)
    p.step("dve", lambda e: e.memset(T["EPSC"][:, 0:1], LNEPS), writes=[B["EPSC"]])
    p.step("dve", lambda e: e.memset(T["EPSC"][:, 1:2], RMSEPS), writes=[B["EPSC"]])

    def mm_group(psum_ap, pairs, psum_buf, reads, tag=""):
        n = len(pairs)

        def fn(e, pairs=tuple(pairs), psum_ap=psum_ap, n=n):
            last = None
            for i, (l, r) in enumerate(pairs):
                last = e.matmul(psum_ap, l, r, start=(i == 0), stop=(i == n - 1))
            return last
        return p.step("pe", fn, reads=reads, writes=[psum_buf])

    def ln_rows(sum_buf, sum_ap, sq_buf, sq_ap, epsc, with_mean, mrow, rrow):
        inv = 1.0 / c.D
        if with_mean:
            p.step("dve", lambda e: e.tensor_scalar_mul(
                rowa[:, mrow, :], sum_ap, inv),
                reads=[sum_buf], writes=[B["ROWA"]])
            p.step("act", lambda e: e.activation(
                rowa[:, mrow + 1, :], rowa[:, mrow, :], AF.Square),
                reads=[B["ROWA"]], writes=[B["ROWA"]])
            p.step("dve", lambda e: e.tensor_scalar(
                rowa[:, mrow + 2, :], sq_ap, inv, None, ALU.mult),
                reads=[sq_buf], writes=[B["ROWA"]])
            p.step("dve", lambda e: e.tensor_sub(
                rowa[:, mrow + 2, :], rowa[:, mrow + 2, :], rowa[:, mrow + 1, :]),
                reads=[B["ROWA"]], writes=[B["ROWA"]])
            vrow = mrow + 2
        else:
            p.step("dve", lambda e: e.tensor_scalar_mul(
                rowa[:, mrow + 2, :], sq_ap, inv),
                reads=[sq_buf], writes=[B["ROWA"]])
            vrow = mrow + 2
        p.step("act", lambda e, vrow=vrow: e.activation(
            rowa[:, rrow, :], rowa[:, vrow, :], AF.Ln,
            bias=T["EPSC"][:, epsc:epsc + 1]),
            reads=[B["ROWA"], B["EPSC"]], writes=[B["ROWA"]])
        p.step("act", lambda e: e.activation(
            rowa[:, rrow, :], rowa[:, rrow, :], AF.Exp, scale=-0.5),
            reads=[B["ROWA"]], writes=[B["ROWA"]])

    def rows_cast(mrow, rrow, with_mean=True):
        if with_mean:
            p.step("dve", lambda e: e.tensor_copy(ap("MBF"), rowa[:, mrow, :]),
                   reads=[B["ROWA"]], writes=[B["MBF"]])
        p.step("dve", lambda e: e.tensor_copy(ap("RBF"), rowa[:, rrow, :]),
               reads=[B["ROWA"]], writes=[B["RBF"]])

    x1sb = {(tt, oc): Buf(f"x1s{tt}_{oc}")
            for tt in range(c.TT) for oc in range(c.FC)}
    scr_b = {}

    preloaded = set()

    def wload(ci, bufname, src_f32, scr_slice, key, q="gp"):
        if key in preloaded:
            preloaded.discard(key)
            return
        if key not in scr_b:
            scr_b[key] = Buf(f"scr{key}")
        al = woverlap[bufname]
        dstap = ap(bufname)
        n1 = dstap.shape[1]
        ns = 4 if ci == 0 else 2
        bounds = sorted(set(n1 * k // ns for k in range(ns + 1)))
        segs = [(a, b) for a, b in zip(bounds, bounds[1:]) if b > a]
        if ci == 0:
            def ldfn(e, o=dstap, i=src_f32, segs=segs):
                return [e.dma_start(out=o[:, a:b], in_=i[:, a:b])
                        for a, b in segs]
            p.step("gp", ldfn, writes=[B[bufname]], aliases=al, dma=bufname,
                   ndma=len(segs))

            def tefn(e, o=scr_slice, i=dstap, segs=segs):
                return [e.dma_start(out=o[:, a:b], in_=i[:, a:b])
                        for a, b in segs]
            p.step("sp", tefn, reads=[B[bufname]], writes=[scr_b[key]],
                   dma=bufname + "s", ndma=len(segs))
        else:
            def rlfn(e, o=dstap, i=scr_slice, segs=segs):
                return [e.dma_start(out=o[:, a:b], in_=i[:, a:b])
                        for a, b in segs]
            p.step(q, rlfn, reads=[scr_b[key]], writes=[B[bufname]],
                   aliases=al, dma=bufname + ("h" if q == "sp" else ""),
                   ndma=len(segs))
    for ci in range(c.NC):
        p.step("dve", lambda e: e.memset(T["FEN"][:], 0.0),
               writes=[B["HID"], B["FEN"]] + tenant_bufs + qkf_bufs
               + [bb for tl in attcb for bb in tl])
        def ln1_stat_unit(tt, fc, ci=ci):
            xs = ("XSQ0", "XSQ1")[fc % 2]
            slot = lnxb[tt][fc % 4]
            dst = T["LNB"][:, fc, tt, :]
            dma("gp", dst, xt[ci, tt, :, fc, :], f"lnx{tt}_{fc % 4}",
                writes=[slot])
            p.step("act", lambda e, dst=dst, xs=xs: e.activation(
                T[xs][:], dst, AF.Square),
                reads=[slot], writes=[B[xs]])

            def stfn(e, dst=dst, xs=xs, fc=fc):
                e.matmul(ps["psQK"][:], T["ONESB"][:], dst,
                         start=(fc == 0), stop=(fc == c.FC - 1))
                return e.matmul(ps["psC"][:], T["ONESB"][:], T[xs][:],
                                start=(fc == 0), stop=(fc == c.FC - 1))
            p.step("pe", stfn, reads=[slot, B[xs], B["ONESB"]],
                   writes=[B["psQK"], B["psC"]])

        def ln1_rows(tt, ci=ci):
            ln_rows(B["psQK"], ps["psQK"][:], B["psC"], ps["psC"][:],
                    0, True, mrow=0, rrow=3)
            p.step("dve", lambda e: e.memset(T["FEN"][:], 0.0),
                   reads=lnxb[tt], writes=[lnbb[tt], B["FEN"]])
            rows_cast(0, 3)

        def ln1_apply_piece(tt, a, ci=ci):
            fq = c.FC // 4
            dst = T["LNB"][:, a * fq:(a + 1) * fq, tt, :]
            mb = T["MBF"][:].unsqueeze(1).broadcast_to([128, fq, TTW])
            rb = T["RBF"][:].unsqueeze(1).broadcast_to([128, fq, TTW])
            p.step("dve", lambda e, dst=dst, mb=mb: e.tensor_tensor(
                dst, dst, mb, ALU.subtract),
                reads=[B["MBF"], lnbb[tt]], writes=[lnbb[tt]])
            p.step("dve", lambda e, dst=dst, rb=rb: e.tensor_tensor(
                dst, dst, rb, ALU.mult),
                reads=[B["RBF"], lnbb[tt]], writes=[lnbb[tt]])

        def ln1_apply(tt, ci=ci):
            ln1_rows(tt, ci)
            for a in range(4):
                ln1_apply_piece(tt, a, ci)
        if ci == 0:
            for pk, bn, sf, ss in ((("wqk", 0), "WQKB1", wqk[0], wqks[0]),
                                   (("wv", 0), "WVB0", wv[0], wvs[0]),
                                   (("wqk", 1), "WQKB", wqk[1], wqks[1])):
                wload(0, bn, sf, ss, pk)
                preloaded.add(pk)
            for tt_ in range(c.TT):
                for fc_ in range(c.FC):
                    ln1_stat_unit(tt_, fc_, 0)
                ln1_apply(tt_, 0)
            emit_table_init()


        def vproj(g):
            wload(ci, "WVB0", wv[g], wvs[g], ("wv", g))
            wvb = T["WVB0"]
            wvbB = B["WVB0"]
            for tc in range(c.NG):
                tt0, o0 = divmod(tc * 128, TTW)
                pairs = [(T["LNB"][:, fc, tt0, o0:o0 + 128], wvb[:, fc, :])
                         for fc in range(c.FC)]
                pb = "psA" if tc % 2 == 0 else "psB"
                mm_group(ps[pb][:], pairs, B[pb], [lnbb[tt0], wvbB])
                p.step("act", lambda e, tc=tc, pb=pb: e.activation(
                    T["VTM"][:, tc, :], ps[pb][:], AF.Silu),
                    reads=[B[pb]], writes=[B["VTM"]])

        qkv_banks = [ps["psQK"][:].rearrange("p (a b) -> p a b", b=BLK),
                     ps["psC"][:].rearrange("p (a b) -> p a b", b=BLK)]
        qkv_bufs = [B["psQK"], B["psC"]]
        ao_banks = [ps["psAO"][:], ps["psD"][:, 0:BLK]]
        ao_bufs = [B["psAO"], B["psD"]]
        qkd_names = ["QKD", "QKD2", "QKD3"]

        def emit_qk(h, b):
            s = h % 2
            i0_ = b * BLK
            bank = qkv_banks[b % 2]

            def qkfn(e, b=b, i0_=i0_, bank=bank, s=s):
                e.matmul(bank[:, 0, :],
                         T[f"KF{s}"][:, (2 * b) * 128:(2 * b + 1) * 128],
                         T[f"QF{s}"][:, i0_:i0_ + BLK], start=True, stop=True)
                return e.matmul(bank[:, 1, :],
                                T[f"KF{s}"][:, (2 * b + 1) * 128:(2 * b + 2) * 128],
                                T[f"QF{s}"][:, i0_:i0_ + BLK], start=True, stop=True)
            tti = i0_ // TTW
            p.step("pe", qkfn, reads=[kftb[s][tti], qftb[s][tti]],
                   writes=[qkv_bufs[b % 2]])
            qkd = qkd_names[(h * c.NB + b) % 3]
            dtb_ = ("DT0", "DT1")[s]
            p.step("dve", lambda e, dtb_=dtb_, qkd=qkd, bank=bank:
                   e.tensor_tensor(T[qkd][:], bank, T[dtb_][:], ALU.mult),
                   reads=[qkv_bufs[b % 2], B[dtb_]], writes=[B[qkd]])

        def proj(h):
            s = h % 2
            dtb, qdc = ("DT0", "QDC0") if s == 0 else ("DT1", "QDC1")
            p.step("gp", lambda e, h=h, dtb=dtb, qdc=qdc: [
                e.dma_start(out=T[dtb][:], in_=tabs[h, :, 0:2, :]),
                e.dma_start(out=T[qdc][:], in_=tabs[h, :, 2, :])],
                reads=[tabs_b[h]], writes=[B[dtb], B[qdc]],
                dma=dtb, ndma=2)
            wqn = ("WQKB1", "WQKB")[h % 2]
            wload(ci, wqn, wqk[h], wqks[h], ("wqk", h), q="sp")
            for tt in range(c.TT):
                qpairs = [(T[wqn][:, fc, 0:128], T["LNB"][:, fc, tt, :])
                          for fc in range(c.FC)]
                kpairs = [(T[wqn][:, fc, 128:256], T["LNB"][:, fc, tt, :])
                          for fc in range(c.FC)]
                mm_group(ps["psA"][:], qpairs, B["psA"], [B[wqn], lnbb[tt]])
                mm_group(ps["psB"][:], kpairs, B["psB"], [B[wqn], lnbb[tt]])
                p.step("act", lambda e, tt=tt, s=s: e.activation(
                    T[f"QF{s}"][:, tt * TTW:(tt + 1) * TTW], ps["psA"][:], AF.Silu),
                    reads=[B["psA"]], writes=[qftb[s][tt]],
                    aliases=[B[f"QF{s}"]])
                p.step("act", lambda e, tt=tt, s=s: e.activation(
                    T[f"KF{s}"][:, tt * TTW:(tt + 1) * TTW], ps["psB"][:], AF.Silu),
                    reads=[B["psB"]], writes=[kftb[s][tt]],
                    aliases=[B[f"KF{s}"]])
                ngh = c.NG // c.TT

                def tf(e, tt=tt, s=s, ngh=ngh):
                    last = None
                    for jt in range(tt * ngh, (tt + 1) * ngh):
                        last = e.transpose(ps["psT"][:, jt, :],
                                           T[f"KF{s}"][:, jt * 128:(jt + 1) * 128],
                                           T["IDNB"][:])
                    return last
                p.step("pe", tf, reads=[kftb[s][tt], B["IDNB"]],
                       writes=[B["psT"]])
            qdb = T[qdc][:].unsqueeze(1).broadcast_to([128, c.NB, BLK])
            p.step("dve", lambda e, qdb=qdb, s=s: e.tensor_tensor(
                T[f"QD{s}"][:].rearrange("p (nb i) -> p nb i", i=BLK),
                T[f"QF{s}"][:].rearrange("p (nb i) -> p nb i", i=BLK),
                qdb, ALU.mult),
                reads=qftb[s][:c.TT] + [B[qdc]], writes=[B[f"QD{s}"]])

            for par in range(2):
                src_ = ps["psT"][:, par::2, :]
                dst_ = T[f"KD{s}"][:, par::2, :]
                p.step("dve", lambda e, src_=src_, dst_=dst_, h=h, par=par:
                       e.tensor_scalar_mul(dst_, src_, T["KDC"][:, h, par:par + 1]),
                       reads=[B["psT"], B["KDC"]], writes=[B[f"KD{s}"]])
            if h % 4 == 0:
                emit_qk(h, 0)
                if c.NB > 1:
                    emit_qk(h, 1)

        def blocks(h):
            s = h % 2
            j = h % 4
            dtb = ("DT0", "DT1")[s]
            for b in range(c.NB):
                i0_ = b * BLK
                qkd = qkd_names[(h * c.NB + b) % 3]
                aob = ao_banks[b % 2]

                def aofn(e, b=b, i0_=i0_, h=h, j=j, qkd=qkd, aob=aob, s=s):
                    e.matmul(aob, T["KVB"][:, h, :],
                             T[f"QD{s}"][:, i0_:i0_ + BLK], start=True, stop=False)
                    e.matmul(aob, T["VTM"][:, 2 * b, j * 128:(j + 1) * 128],
                             T[qkd][:, 0, :], start=False, stop=False)
                    return e.matmul(aob,
                                    T["VTM"][:, 2 * b + 1, j * 128:(j + 1) * 128],
                                    T[qkd][:, 1, :], start=False, stop=True)
                p.step("pe", aofn, reads=[kvb_b[h], B[f"QD{s}"], B["VTM"], B[qkd]],
                       writes=[ao_bufs[b % 2]])
                tt0 = b // (TTW // BLK)
                half = b % (TTW // BLK)
                p.step("act", lambda e, h=h, tt0=tt0, half=half, aob=aob:
                       e.activation(
                           T["ATT"][:, h, tt0, half * BLK:(half + 1) * BLK],
                           aob, AF.Copy),
                       reads=[ao_bufs[b % 2]], writes=[attcb[tt0][h]])

                def kvfn(e, b=b, j=j, s=s):
                    e.matmul(ps["psKV"][:], T[f"KD{s}"][:, 2 * b, :],
                             T["VTM"][:, 2 * b, j * 128:(j + 1) * 128],
                             start=True, stop=False)
                    return e.matmul(ps["psKV"][:], T[f"KD{s}"][:, 2 * b + 1, :],
                                    T["VTM"][:, 2 * b + 1, j * 128:(j + 1) * 128],
                                    start=False, stop=True)
                p.step("pe", kvfn, reads=[B[f"KD{s}"], B["VTM"]], writes=[B["psKV"]])
                p.step("dve", lambda e, h=h: e.scalar_tensor_tensor(
                    T["KVB"][:, h, :], T["KVB"][:, h, :], T["BD"][:, h:h + 1],
                    ps["psKV"][:], ALU.mult, ALU.add),
                    reads=[B["psKV"], B["BD"], kvb_b[h]], writes=[kvb_b[h]])
                if b + 2 < c.NB:
                    emit_qk(h, b + 2)
                elif h + 1 < c.H and (h + 1) % 4 != 0:
                    emit_qk(h + 1, b + 2 - c.NB)
        proj(0)
        vproj(0)
        for h in range(c.H):
            nh = h + 1
            if nh < c.H and nh % 4 != 0:
                proj(nh)
                blocks(h)
            elif nh < c.H:
                blocks(h)
                proj(nh)
                vproj(nh // 4)
            else:
                blocks(h)

        efg_seq = [0]

        def emit_gate(tt):
            for oc in range(c.FC):
                if oc % 2 == 0:
                    wbn = ("WB0", "WB1", "WB2", "WB3")[efg_seq[0] % 4]
                    efg_seq[0] += 1
                    wload(ci, wbn, wg[oc // 2], wgs[oc // 2], ("wg", oc // 2))
                    emit_gate.cur = wbn
                wbn = emit_gate.cur
                sub = oc % 2
                xs = ("XSQ0", "XSQ1")[oc % 2]
                p.step("act", lambda e, xs=xs, oc=oc, tt=tt: e.activation(
                    T[xs][:], T["ATT"][:, oc, tt, :], AF.Square),
                    reads=[attcb[tt][oc]], writes=[B[xs]])
                pairs = [(T[wbn][:, fc, sub * 128:(sub + 1) * 128],
                          T["LNB"][:, fc, tt, :]) for fc in range(c.FC)]
                pb = ("psA", "psB", "psC")[oc % 3]
                mm_group(ps[pb][:], pairs, B[pb], [B[wbn], lnbb[tt]])
                p.step("pe", lambda e, xs=xs, oc=oc: e.matmul(
                    ps["psD"][:], T["ONESB"][:], T[xs][:],
                    start=(oc == 0), stop=(oc == c.FC - 1)),
                    reads=[B[xs], B["ONESB"]], writes=[B["psD"]])
                p.step("act", lambda e, oc=oc, pb=pb: e.activation(
                    T["GT"][:, oc, :], ps[pb][:], AF.Sigmoid),
                    reads=[B[pb]], writes=[B["GT"]])
                p.step("dve", lambda e, oc=oc, tt=tt: e.scalar_tensor_tensor(
                    T["ATT"][:, oc, tt, :], T["ATT"][:, oc, tt, :],
                    T["NW"][:, oc:oc + 1], T["GT"][:, oc, :],
                    ALU.mult, ALU.mult),
                    reads=[B["GT"], attcb[tt][oc]], writes=[attcb[tt][oc]])

        def emit_rows_p(tt, defer=False):
            p.step("dve", lambda e: e.tensor_scalar_mul(
                rowa[:, 0, :], ps["psD"][:], 1.0 / c.D),
                reads=[B["psD"]], writes=[B["ROWA"]])
            p.step("act", lambda e: e.activation(
                rowa[:, 1, :], rowa[:, 0, :], AF.Ln,
                bias=T["EPSC"][:, 1:2]),
                reads=[B["ROWA"], B["EPSC"]], writes=[B["ROWA"]])
            p.step("act", lambda e: e.activation(
                rowa[:, 1, :], rowa[:, 1, :], AF.Exp, scale=-0.5),
                reads=[B["ROWA"]], writes=[B["ROWA"]])
            rows_cast(0, 1, with_mean=False)
            nq = c.H // 4
            pieces = []
            for a in range(4):
                attv = T["ATT"][:, a * nq:(a + 1) * nq, tt, :]
                rbb = T["RBF"][:].unsqueeze(1).broadcast_to([128, nq, TTW])

                def mk(attv=attv, rbb=rbb, a=a, tt=tt):
                    p.step("dve", lambda e, attv=attv, rbb=rbb: e.tensor_tensor(
                        attv, attv, rbb, ALU.mult), reads=[B["RBF"]],
                        writes=attcb[tt][a * nq:(a + 1) * nq])
                pieces.append(mk)
            if not defer:
                for mk in pieces:
                    mk()
                return []
            return pieces

        def emit_out(tt, inject=(), inj_start=4, inj_step=2, defer_ln2=False):
            inj = list(inject)

            def emit_l2stat(oc):
                xs = ("XSQ0", "XSQ1", "XSQ2")[oc % 3]

                def l2fn(e, oc=oc, xs=xs):
                    e.matmul(ps["psQK"][:], T["ONESB"][:], T["X1B"][:, oc, :],
                             start=(oc == 0), stop=(oc == c.FC - 1))
                    return e.matmul(ps["psD"][:], T["ONESB"][:], T[xs][:],
                                    start=(oc == 0), stop=(oc == c.FC - 1))
                p.step("pe", l2fn, reads=[x1cb[oc], B[xs], B["ONESB"]],
                       writes=[B["psQK"], B["psD"]])
            for oc in range(c.FC):
                if inj and oc >= inj_start and (oc - inj_start) % inj_step == 0:
                    inj.pop(0)()
                if oc % 2 == 0:
                    wbn = ("WB0", "WB1", "WB2", "WB3")[efg_seq[0] % 4]
                    efg_seq[0] += 1
                    wload(ci, wbn, wo[oc // 2], wos[oc // 2], ("wo", oc // 2))
                    emit_out.cur = wbn
                wbn = emit_out.cur
                sub = oc % 2
                stg = ("STG0", "STG1", "STG2", "STG3")[oc % 4]
                if ci == 0:
                    dma("sp", ap(stg), xt[ci, tt, :, oc, :], stg,
                        writes=[B[stg]])
                else:
                    dma("gp", ap(stg), xt[ci, tt, :, oc, :], stg + "g",
                        writes=[B[stg]])
                pairs = [(T[wbn][:, fc, sub * 128:(sub + 1) * 128],
                          T["ATT"][:, fc, tt, :]) for fc in range(c.FC)]
                pb = ("psA", "psB", "psC")[oc % 3]
                mm_group(ps[pb][:], pairs, B[pb], [B[wbn]] + attcb[tt])
                p.step("dve", lambda e, stg=stg, pb=pb: e.tensor_tensor(
                    ap(stg), ap(stg), ps[pb][:], ALU.add),
                    reads=[B[pb], B[stg]], writes=[B[stg]])
                p.step("act", lambda e, stg=stg, oc=oc: e.activation(
                    T["X1B"][:, oc, :], ap(stg), AF.Copy),
                    reads=[B[stg]], writes=[x1cb[oc]])
                xs = ("XSQ0", "XSQ1", "XSQ2")[oc % 3]
                p.step("act", lambda e, stg=stg, xs=xs: e.activation(
                    T[xs][:], ap(stg), AF.Square),
                    reads=[B[stg]], writes=[B[xs]])
                dma("sp", x1s[ci, tt, :, oc, :], ap(stg), stg, reads=[B[stg]],
                    writes=[x1sb[(tt, oc)]])
                if oc > 1:
                    emit_l2stat(oc - 2)
            for fn in inj:
                fn()
            inj = []
            emit_l2stat(c.FC - 2)
            emit_l2stat(c.FC - 1)
            ln_rows(B["psQK"], ps["psQK"][:], B["psD"], ps["psD"][:],
                    0, True, mrow=0, rrow=3)
            rows_cast(0, 3)
            fq = c.FC // 4
            ln2p = []
            for a in range(4):
                dsta = T["LNB"][:, a * fq:(a + 1) * fq, tt, :]
                x1a = T["X1B"][:, a * fq:(a + 1) * fq, :]
                mba = T["MBF"][:].unsqueeze(1).broadcast_to([128, fq, TTW])
                rba = T["RBF"][:].unsqueeze(1).broadcast_to([128, fq, TTW])

                def mksub(dsta=dsta, x1a=x1a, mba=mba, a=a, tt=tt):
                    p.step("dve", lambda e, dsta=dsta, x1a=x1a, mba=mba:
                           e.tensor_tensor(dsta, x1a, mba, ALU.subtract),
                           reads=x1cb[a * fq:(a + 1) * fq] + [B["MBF"]],
                           writes=[lnbb[tt]])

                def mkmul(dsta=dsta, rba=rba, tt=tt):
                    p.step("dve", lambda e, dsta=dsta, rba=rba:
                           e.tensor_tensor(dsta, dsta, rba, ALU.mult),
                           reads=[B["RBF"], lnbb[tt]], writes=[lnbb[tt]])
                ln2p += [mksub, mkmul]
            if defer_ln2:
                return ln2p
            for fn in ln2p:
                fn()
        emit_gate(0)
        emit_rows_p(0)
        if c.TT > 1:
            emit_gate(1)
            pieces = emit_rows_p(1, defer=True)
            ln2p = emit_out(0, inject=pieces, defer_ln2=True)
            emit_out(1, inject=ln2p, inj_start=0, inj_step=1)
        else:
            emit_out(0)

        dma("gp", ap("LGA"), lga, "LGA", writes=[B["LGA"]],
            aliases=woverlap["LGA"])
        dma("gp", ap("LUA"), lua, "LUA", writes=[B["LUA"]],
            aliases=woverlap["LUA"])
        ngr = TTW // 128
        abank = (("psA", "psB"), ("psC", "psD"))

        def prep_router(tt):
            rpairs = [(T["WRB"][:, fc, :], T["LNB"][:, fc, tt, :])
                      for fc in range(c.FC)]
            mm_group(ps["psC"][0:8, :], rpairs, B["psC"], [B["WRB"], lnbb[tt]])
            p.step("act", lambda e, tt=tt: e.activation(
                T["SC"][:, tt * TTW:(tt + 1) * TTW], ps["psC"][0:8, :], AF.Sigmoid),
                reads=[B["psC"]], writes=[B["SC"]])

        def prep_top2(tt):
            def scfn(e, tt=tt):
                last = None
                for gidx in range(ngr):
                    last = e.transpose(
                        ps["psT"][:, 0, gidx * 8:(gidx + 1) * 8],
                        T["SC"][0:8, tt * TTW + gidx * 128: tt * TTW + (gidx + 1) * 128],
                        T["IDNB"][0:8, 0:8])
                return last
            p.step("pe", scfn, reads=[B["SC"], B["IDNB"]], writes=[B["psT"]])
            p.step("dve", lambda e, tt=tt: e.tensor_copy(
                T["SCT"][:, tt], ps["psT"][:, 0, 0:ngr * 8].rearrange(
                    "p (g e) -> p g e", e=8)),
                reads=[B["psT"]], writes=[B["SCT"]])
            bb = T["BIAS"][:].unsqueeze(1).broadcast_to([128, ngr, 8])
            p.step("dve", lambda e, tt=tt, bb=bb: e.tensor_tensor(
                T["BSD"][:], T["SCT"][:, tt], bb, ALU.add),
                reads=[B["SCT"], B["BIAS"]], writes=[B["BSD"]])
            p.step("dve", lambda e: e.reduce_max(T["M1"][:], T["BSD"][:], axis=AX.X),
                   reads=[B["BSD"]], writes=[B["M1"]])
            m1b = T["M1"][:].unsqueeze(2).broadcast_to([128, ngr, 8])
            p.step("dve", lambda e, m1b=m1b: e.tensor_tensor(
                T["TOP"][:], T["BSD"][:], m1b, ALU.is_ge),
                reads=[B["BSD"], B["M1"]], writes=[B["TOP"]])
            p.step("dve", lambda e: e.scalar_tensor_tensor(
                T["BSD"][:], T["TOP"][:], -1e9, T["BSD"][:], ALU.mult, ALU.add),
                reads=[B["TOP"], B["BSD"]], writes=[B["BSD"]])
            p.step("dve", lambda e: e.reduce_max(T["M1"][:], T["BSD"][:], axis=AX.X),
                   reads=[B["BSD"]], writes=[B["M1"]])
            p.step("dve", lambda e, m1b=m1b: e.tensor_tensor(
                T["WTM"][:], T["BSD"][:], m1b, ALU.is_ge),
                reads=[B["BSD"], B["M1"]], writes=[B["WTM"]])
            p.step("dve", lambda e: e.tensor_add(
                T["TOP"][:], T["TOP"][:], T["WTM"][:]),
                reads=[B["WTM"], B["TOP"]], writes=[B["TOP"]])
            p.step("dve", lambda e, tt=tt: e.tensor_tensor(
                T["WTM"][:], T["TOP"][:], T["SCT"][:, tt], ALU.mult),
                reads=[B["TOP"], B["SCT"]], writes=[B["WTM"]])
            p.step("dve", lambda e: e.reduce_sum(T["M1"][:], T["WTM"][:], axis=AX.X),
                   reads=[B["WTM"]], writes=[B["M1"]])
            p.step("dve", lambda e: e.tensor_scalar_add(
                T["M1"][:], T["M1"][:], TOPEPS),
                reads=[B["M1"]], writes=[B["M1"]])
            p.step("dve", lambda e: e.reciprocal(T["M1"][:], T["M1"][:]),
                   reads=[B["M1"]], writes=[B["M1"]])
            p.step("dve", lambda e, m1b=m1b, tt=tt: e.tensor_tensor(
                T["WTM"][:], T["WTM"][:], m1b, ALU.mult),
                reads=[B["M1"], B["WTM"]], writes=[B["WTM"]])
        def prep_wet(tt):
            def wtfn(e):
                last = None
                for gidx in range(ngr):
                    last = e.transpose(
                        ps["psQK"][0:8, gidx * 128:(gidx + 1) * 128],
                        T["WTM"][:, gidx, :], T["IDNF"][:])
                return last
            p.step("pe", wtfn, reads=[B["WTM"], B["IDNF"]], writes=[B["psQK"]])
            p.step("dve", lambda e: e.tensor_copy(T["WETB"][:], ps["psQK"][0:8, :]),
                   reads=[B["psQK"]], writes=[B["WETB"]])
            p.step("pe", lambda e: e.matmul(
                ps["psQK"][0:64, :], T["E8"][:], T["WETB"][:], start=True, stop=True),
                reads=[B["E8"], B["WETB"]], writes=[B["psQK"]])
            p.step("dve", lambda e, tt=tt: e.tensor_copy(
                T["WREP"][:, tt, :], ps["psQK"][0:64, :]),
                reads=[B["psQK"]], writes=[B["WREP"]])

        def prep_a(tt):
            ga, ua = abank[tt]
            gpairs = [(T["LGA"][:, fc, :], T["LNB"][:, fc, tt, :])
                      for fc in range(c.FC)]
            upairs = [(T["LUA"][:, fc, :], T["LNB"][:, fc, tt, :])
                      for fc in range(c.FC)]
            mm_group(ps[ga][0:72, :], gpairs, B[ga], [B["LGA"], lnbb[tt]])
            mm_group(ps[ua][0:72, :], upairs, B[ua], [B["LUA"], lnbb[tt]])

        def prep_m(tt):
            ga, ua = abank[tt]
            p.step("dve", lambda e, tt=tt, ga=ga: e.tensor_tensor(
                T["MG"][0:64, tt, :], ps[ga][0:64, :], T["WREP"][:, tt, :],
                ALU.mult), reads=[B[ga], B["WREP"]], writes=[mgb[tt]])
            p.step("dve", lambda e, tt=tt, ga=ga: e.tensor_scalar_mul(
                T["MG"][64:72, tt, :], ps[ga][64:72, :], 0.125),
                reads=[B[ga]], writes=[mgb[tt]])
            p.step("dve", lambda e, tt=tt, ua=ua: e.tensor_tensor(
                T["MU"][0:64, tt, :], ps[ua][0:64, :], T["WREP"][:, tt, :],
                ALU.mult), reads=[B[ua], B["WREP"]], writes=[mub[tt]])
            p.step("dve", lambda e, tt=tt, ua=ua: e.tensor_scalar_mul(
                T["MU"][64:72, tt, :], ps[ua][64:72, :], 0.125),
                reads=[B[ua]], writes=[mub[tt]])
        prep_router(0)
        prep_a(0)
        prep_top2(0)
        if c.TT > 1:
            prep_router(1)
            prep_a(1)
        prep_wet(0)
        prep_m(0)
        if c.TT > 1:
            prep_top2(1)
        p.step("dve", lambda e: e.memset(T["FEN"][:], 0.0),
               writes=[B["HID"], B["FEN"]] + tenant_bufs + qkf_bufs
               + [bb for tl in attcb for bb in tl])
        dma("gp", ap("LDA"), lda, "LDA", writes=[B["LDA"]],
            aliases=woverlap["LDA"])
        for dc in range(c.DC):
            wgb = ("WGT0", "WGT1")[dc % 2]
            wub = ("WUP0", "WUP1")[dc % 2]
            wload(ci, wgb, wgate[dc], wgates[dc], ("wgate", dc))
            wload(ci, wub, wup[dc], wups[dc], ("wup", dc))
            for tt in range(c.TT):
                if dc == 0 and tt == 1:
                    prep_wet(1)
                    prep_m(1)
                gp_ = ("psA", "psC")[tt]
                up_ = ("psB", "psD")[tt]
                gpairs = [(T[wgb][:, fc, :], T["LNB"][:, fc, tt, :])
                          for fc in range(c.FC)]
                gpairs.append((T[wgb][0:72, c.FC, :], T["MG"][:, tt, :]))
                upairs = [(T[wub][:, fc, :], T["LNB"][:, fc, tt, :])
                          for fc in range(c.FC)]
                upairs.append((T[wub][0:72, c.FC, :], T["MU"][:, tt, :]))
                mm_group(ps[gp_][:], gpairs, B[gp_], [B[wgb], lnbb[tt], mgb[tt]])
                mm_group(ps[up_][:], upairs, B[up_], [B[wub], lnbb[tt], mub[tt]])
                hs = ("HS0", "HS1")[tt]
                p.step("act", lambda e, hs=hs, gp_=gp_: e.activation(
                    ap(hs), ps[gp_][:], AF.Silu), reads=[B[gp_]], writes=[B[hs]])
                p.step("dve", lambda e, hs=hs, up_=up_, dc=dc, tt=tt: e.tensor_tensor(
                    T["HID"][:, dc, tt, :], ps[up_][:], ap(hs), ALU.mult),
                    reads=[B[up_], B[hs]], writes=[B["HID"]])

        for tt in range(c.TT):
            dpairs = [(T["LDA"][:, fc, :], T["HID"][:, fc, tt, :])
                      for fc in range(c.DC)]
            mm_group(ps["psA"][0:72, :], dpairs, B["psA"], [B["LDA"], B["HID"]])
            p.step("dve", lambda e, tt=tt: e.tensor_tensor(
                T["MD"][0:64, tt, :], ps["psA"][0:64, :], T["WREP"][:, tt, :],
                ALU.mult), reads=[B["psA"], B["WREP"]], writes=[B["MD"]])
            p.step("dve", lambda e, tt=tt: e.tensor_scalar_mul(
                T["MD"][64:72, tt, :], ps["psA"][64:72, :], 0.125),
                reads=[B["psA"]], writes=[B["MD"]])
        nxt = ci + 1 < c.NC
        ride = []
        if nxt and c.TT > 1:
            for att in range(c.TT):
                for fc in range(c.FC):
                    ride.append(lambda att=att, fc=fc:
                                ln1_stat_unit(att, fc, ci + 1))
                ride.append(lambda att=att: ln1_rows(att, ci + 1))
                for a in range(4):
                    ride.append(lambda att=att, a=a:
                                ln1_apply_piece(att, a, ci + 1))
        rpos = [0]
        for oc in range(c.FC):
            wdb = ("WDN0", "WDN1")[oc % 2]
            wload(ci, wdb, wdown[oc], wdowns[oc], ("wdown", oc))
            for tt in range(c.TT):
                stg = ("STG2", "STG3")[tt]
                dma("sp", ap(stg), x1s[ci, tt, :, oc, :], stg,
                    reads=[x1sb[(tt, oc)]], writes=[B[stg]])
                pairs = [(T[wdb][:, fc, :], T["HID"][:, fc, tt, :])
                         for fc in range(c.DC)]
                pairs.append((T[wdb][0:72, c.DC, :], T["MD"][:, tt, :]))
                pb = ("psA", "psB", "psD")[(2 * oc + tt) % 3]
                mm_group(ps[pb][:], pairs, B[pb], [B[wdb], B["HID"], B["MD"]])
                p.step("dve", lambda e, stg=stg, pb=pb: e.tensor_tensor(
                    ap(stg), ap(stg), ps[pb][:], ALU.add),
                    reads=[B[pb], B[stg]], writes=[B[stg]])
                dma("sp", out[ci, tt, :, oc, :], ap(stg), stg, reads=[B[stg]])
            if nxt and oc == c.FC - 4:
                wload(ci + 1, "WQKB1", wqk[0], wqks[0], ("wqk", 0), q="sp")
                preloaded.add(("wqk", 0))
            if nxt:
                if c.TT > 1:
                    take = ride[rpos[0]:rpos[0] + 4]
                    rpos[0] += len(take)
                    for it in take:
                        it()
                else:
                    ln1_stat_unit(0, oc, ci + 1)
                    if oc == c.FC - 1:
                        ln1_apply(0, ci + 1)

    p.emit()
    nc.finalize()
    return nc


def kernel(**inputs):
    x = np.asarray(inputs["L0_attn_hidden_states"])
    Bsz, NT, D = x.shape
    H = int(np.asarray(inputs["L0_attn_slope_rate"]).shape[0])
    DFF = int(np.asarray(inputs["L0_ffn_W_gate"]).shape[1])
    cfg = Cfg(D=D, H=H, NT=NT, DFF=DFF, chunk=1024)
    shared, xs = prep_inputs(cfg, inputs)
    nc = build_nc(cfg)
    in_maps = [dict(shared, xt=xs[b]) for b in range(Bsz)]
    res = run_bass_kernel_spmd(nc, in_maps, core_ids=list(range(Bsz)))
    outs = [r["out"] for r in res.results]
    return unshard_out(cfg, outs)


# version 44 (final)
# speedup vs baseline: 1.1171x; 1.0438x over previous
import numpy as np
import ml_dtypes
import concourse.bass as bass
import concourse.mybir as mybir
from concourse.bass_utils import run_bass_kernel_spmd

F32 = mybir.dt.float32
BF16 = mybir.dt.bfloat16
AF = mybir.ActivationFunctionType
ALU = mybir.AluOpType
AX = mybir.AxisListType
BF = ml_dtypes.bfloat16

LNEPS = 1e-5
RMSEPS = 1e-6
TOPEPS = 1e-8
BLK = 256
TTW = 512


class Cfg:
    def __init__(self, D, H, NT, DFF, chunk=1024):
        self.D = D
        self.H = H
        self.DH = 128
        self.NT = NT
        self.DFF = DFF
        self.E = 8
        self.R = 8
        self.LR = 72
        self.chunk = min(chunk, NT)
        assert self.chunk % TTW == 0 and NT % self.chunk == 0
        self.NC = NT // self.chunk
        self.TT = self.chunk // TTW
        self.NB = self.chunk // BLK
        self.NG = self.chunk // 128
        self.FC = D // 128
        self.DC = DFF // 128
        assert D % 128 == 0 and DFF % 128 == 0 and H % 4 == 0
        assert D == H * 128


def _tile_w(wt, pw):
    F, O = wt.shape
    a = wt.reshape(F // 128, 128, O // pw, pw).transpose(2, 1, 0, 3)
    return np.ascontiguousarray(a, dtype=np.float32)


def _pack_lora(wt, bmat, pw=128):
    F, O = wt.shape
    a = _tile_w(wt, pw)
    bpad = np.zeros((O // pw, 128, 1, pw), np.float32)
    b = bmat.reshape(72, O // pw, pw).transpose(1, 0, 2)
    bpad[:, :72, 0, :] = b
    return np.ascontiguousarray(np.concatenate([a, bpad], axis=2))


def prep_inputs(cfg, ins):
    c = cfg
    f32 = np.float32
    wqkv = np.asarray(ins["L0_attn_w_qkv"], f32)
    wqkvT = np.ascontiguousarray(wqkv.T)
    D, H = c.D, c.H
    qk = np.empty((H, 128, c.FC, 256), f32)
    for h in range(H):
        qh = wqkvT[:, h * 128:(h + 1) * 128]
        kh = wqkvT[:, D + h * 128: D + (h + 1) * 128]
        pan = np.concatenate([qh, kh], axis=1)
        qk[h] = pan.reshape(c.FC, 128, 256).transpose(1, 0, 2)
    wv = np.empty((H // 4, 128, c.FC, 512), f32)
    vT = wqkvT[:, 2 * D:3 * D]
    for g in range(H // 4):
        pan = vT[:, g * 512:(g + 1) * 512]
        wv[g] = pan.reshape(c.FC, 128, 512).transpose(1, 0, 2)

    wg = _tile_w(np.ascontiguousarray(np.asarray(ins["L0_attn_w_gate"], f32).T), 256)
    wo = _tile_w(np.ascontiguousarray(np.asarray(ins["L0_attn_w_out"], f32).T), 256)

    def lstack(A, sA):
        A = np.asarray(A, f32); sA = np.asarray(sA, f32)
        a = A.transpose(1, 0, 2).reshape(A.shape[1], 64)
        st = np.concatenate([a, sA], axis=1)
        return np.ascontiguousarray(
            st.reshape(st.shape[0] // 128, 128, 72).transpose(1, 0, 2))

    def bstack(B, sB):
        B = np.asarray(B, f32).reshape(64, -1)
        return np.concatenate([B, np.asarray(sB, f32)], axis=0)

    wgate = _pack_lora(np.asarray(ins["L0_ffn_W_gate"], f32),
                       bstack(ins["L0_ffn_B_gate"], ins["L0_ffn_sB_gate"]))
    wup = _pack_lora(np.asarray(ins["L0_ffn_W_up"], f32),
                     bstack(ins["L0_ffn_B_up"], ins["L0_ffn_sB_up"]))
    wdown = _pack_lora(np.asarray(ins["L0_ffn_W_down"], f32),
                       bstack(ins["L0_ffn_B_down"], ins["L0_ffn_sB_down"]))

    wr = np.ascontiguousarray(
        np.asarray(ins["L0_ffn_W_router"], f32).reshape(c.FC, 128, 8).transpose(1, 0, 2))

    nw = np.asarray(ins["L0_attn_norm_w"], f32).reshape(c.FC, 128).T

    i0 = np.arange(BLK, dtype=f32)
    posm = (i0[None, :] - i0[:, None])
    posm = np.where(posm >= 0, posm, 1e9).astype(f32)
    posm = np.ascontiguousarray(posm.reshape(2, 128, BLK).transpose(1, 0, 2))
    rowi = np.broadcast_to(i0[None, :] + 1.0, (128, BLK)).astype(f32)
    rowkd = np.empty((128, 2), f32)
    for par in range(2):
        rowkd[:, par] = 255.0 - (par * 128 + np.arange(128, dtype=f32))
    e8 = np.zeros((8, 64), f32)
    for e in range(8):
        e8[e, e * 8:(e + 1) * 8] = 0.125
    shared = {
        "wqk": qk, "wv": wv, "wg": wg, "wo": wo,
        "wgate": wgate, "wup": wup, "wdown": wdown, "wr": wr,
        "lga": lstack(ins["L0_ffn_A_gate"], ins["L0_ffn_sA_gate"]),
        "lua": lstack(ins["L0_ffn_A_up"], ins["L0_ffn_sA_up"]),
        "lda": lstack(ins["L0_ffn_A_down"], ins["L0_ffn_sA_down"]),
        "slope": np.asarray(ins["L0_attn_slope_rate"], f32).reshape(1, H),
        "ebias": np.asarray(ins["L0_ffn_expert_bias"], f32).reshape(1, 8),
        "normw": np.ascontiguousarray(nw),
        "idnb": np.eye(128, dtype=f32).astype(BF),
        "idnf": np.eye(128, dtype=f32),
        "onesb": np.ones((128, 128), f32).astype(BF),
        "onesf": np.ones((1, 128), f32),
        "rowi": rowi, "posm": posm, "rowkd": rowkd,
        "e8": e8.astype(BF),
    }
    x = np.asarray(ins["L0_attn_hidden_states"], f32)
    xs = []
    for b in range(x.shape[0]):
        xt = np.ascontiguousarray(x[b].T)
        xt = xt.reshape(c.FC, 128, c.NC, c.TT, TTW).transpose(2, 3, 1, 0, 4)
        xs.append(np.ascontiguousarray(xt))
    return shared, xs


def unshard_out(cfg, outs):
    c = cfg
    res = []
    for o in outs:
        a = o.transpose(3, 2, 0, 1, 4).reshape(c.D, c.NT)
        res.append(a.T)
    return np.ascontiguousarray(np.stack(res)).astype(np.float32)


class Buf:
    __slots__ = ("w", "r", "name")

    def __init__(self, name=""):
        self.name = name
        self.w = None
        self.r = []


class Prog:
    ENGS = ("pe", "act", "dve", "sp", "gp")

    def __init__(self, nc):
        self.nc = nc
        self.stream = {e: [] for e in self.ENGS}
        self.semh = {s: nc.alloc_semaphore("S" + s)
                     for s in ("pe", "act", "dve", "gpc")}
        self.cnt = {s: 0 for s in self.semh}
        self.hw = {}
        self.esem = {"pe": "pe", "act": "act", "dve": "dve", "gp": "gpc"}

    def dsem(self, name):
        if name not in self.semh:
            self.semh[name] = self.nc.alloc_semaphore("D" + name)
            self.cnt[name] = 0
        return name

    def step(self, eng, fn, reads=(), writes=(), dma=None, ndma=1, aliases=()):
        toks = []
        for b in reads:
            if b.w is not None:
                toks.append(b.w)
        for b in writes:
            if b.w is not None:
                toks.append(b.w)
            toks.extend(b.r)
        for b in aliases:
            if b.w is not None:
                toks.append(b.w)
            toks.extend(b.r)
        need = {}
        for s, v in toks:
            if need.get(s, -1) < v:
                need[s] = v
        waits = []
        for s, v in need.items():
            key = (eng, s)
            if self.hw.get(key, 0) >= v:
                continue
            self.hw[key] = v
            waits.append((s, v))
        if dma is not None:
            sname = self.dsem(dma)
            amt = 16
            self.cnt[sname] += 16 * ndma
        else:
            sname = self.esem[eng]
            amt = 1
            self.cnt[sname] += 1
        tok = (sname, self.cnt[sname])
        self.stream[eng].append((waits, fn, sname, amt))
        for b in reads:
            b.r.append(tok)
        for b in writes:
            b.w = tok
            b.r = []
        for b in aliases:
            b.r.append(tok)
        return tok

    def emit(self):
        nc = self.nc
        semh = self.semh
        stream = self.stream
        finals = dict(self.cnt)

        with nc.Block() as block:
            def make(key):
                def body(e):
                    for waits, fn, sname, amt in stream[key]:
                        for s, v in waits:
                            e.wait_ge(semh[s], v)
                        last = fn(e)
                        if isinstance(last, list):
                            for ins in last:
                                ins.then_inc(semh[sname], amt)
                        else:
                            assert last is not None, f"step on {key} -> None"
                            last.then_inc(semh[sname], amt)
                    if key == "sp":
                        for s, v in finals.items():
                            if v > self.hw.get(("sp", s), 0):
                                e.wait_ge(semh[s], v)
                return body
            block.tensor(make("pe"))
            block.scalar(make("act"))
            block.vector(make("dve"))
            block.sync(make("sp"))
            block.gpsimd(make("gp"))


def build_nc(cfg):
    c = cfg
    nc = bass.Bass("TRN2", target_bir_lowering=False)
    p = Prog(nc)

    def din(name, shape, dt=F32):
        return nc.dram_tensor(name, list(shape), dt, kind="ExternalInput").ap()

    xt = din("xt", (c.NC, c.TT, 128, c.FC, TTW))
    wqk = din("wqk", (c.H, 128, c.FC, 256))
    wv = din("wv", (c.H // 4, 128, c.FC, 512))
    wg = din("wg", (c.FC // 2, 128, c.FC, 256))
    wo = din("wo", (c.FC // 2, 128, c.FC, 256))
    wgate = din("wgate", (c.DC, 128, c.FC + 1, 128))
    wup = din("wup", (c.DC, 128, c.FC + 1, 128))
    wdown = din("wdown", (c.FC, 128, c.DC + 1, 128))
    wr = din("wr", (128, c.FC, 8))
    lga = din("lga", (128, c.FC, 72))
    lua = din("lua", (128, c.FC, 72))
    lda = din("lda", (128, c.DC, 72))
    slope = din("slope", (1, c.H))
    ebias = din("ebias", (1, 8))
    normw = din("normw", (128, c.FC))
    idnb = din("idnb", (128, 128), BF16)
    idnf = din("idnf", (128, 128))
    onesb = din("onesb", (128, 128), BF16)
    onesf = din("onesf", (1, 128))
    rowi = din("rowi", (128, BLK))
    posm = din("posm", (128, 2, BLK))
    rowkd = din("rowkd", (128, 2))
    e8 = din("e8", (8, 64), BF16)

    out = nc.dram_tensor("out", [c.NC, c.TT, 128, c.FC, TTW], F32,
                         kind="ExternalOutput").ap()
    tabs = nc.dram_tensor("tabs", [c.H, 128, 3, BLK], BF16, kind="Internal").ap()
    wqks = nc.dram_tensor("wqks", [c.H, 128, c.FC, 256], BF16, kind="Internal").ap()
    wvs = nc.dram_tensor("wvs", [c.H // 4, 128, c.FC, 512], BF16, kind="Internal").ap()
    wgs = nc.dram_tensor("wgs", [c.FC // 2, 128, c.FC, 256], BF16, kind="Internal").ap()
    wos = nc.dram_tensor("wos", [c.FC // 2, 128, c.FC, 256], BF16, kind="Internal").ap()
    wgates = nc.dram_tensor("wgates", [c.DC, 128, c.FC + 1, 128], BF16, kind="Internal").ap()
    wups = nc.dram_tensor("wups", [c.DC, 128, c.FC + 1, 128], BF16, kind="Internal").ap()
    wdowns = nc.dram_tensor("wdowns", [c.FC, 128, c.DC + 1, 128], BF16, kind="Internal").ap()
    x1s = nc.dram_tensor("x1s", [c.NC, c.TT, 128, c.FC, TTW], F32,
                         kind="Internal").ap()

    sb = {}
    bufs = {}

    def S(name, shape, dt=F32, offset=None):
        if offset is None:
            h = nc.alloc_sbuf_tensor(name, list(shape), dt)
        else:
            h = nc.alloc_sbuf_tensor_at(name, list(shape), dt, offset=offset)
        sb[name] = h
        bufs[name] = Buf(name)
        return h

    S("QF0", (128, c.chunk), BF16)
    S("QD0", (128, c.chunk), BF16)
    S("KF0", (128, c.chunk), BF16)
    S("KD0", (128, c.NG, 128), BF16)
    S("QKD", (128, 2, BLK), BF16)
    S("QKD2", (128, 2, BLK), BF16)
    S("QKD3", (128, 2, BLK), BF16)
    S("KVB", (128, c.H, 128), BF16)
    S("DT0", (128, 2, BLK), BF16)
    S("DT1", (128, 2, BLK), BF16)
    S("QDC0", (128, BLK), BF16)
    S("QDC1", (128, BLK), BF16)
    S("KDC", (128, c.H, 2))
    S("SB", (128, c.H))
    S("BD", (128, c.H))
    S("BIAS", (128, 8))
    S("NW", (128, c.FC))
    S("ONESB", (128, 128), BF16)
    S("IDNB", (128, 128), BF16)
    S("IDNF", (128, 128))
    S("ONESF", (1, 128))
    S("ROWKD", (128, 2))
    S("ROWA", (128, 4, TTW))
    S("MBF", (128, TTW), BF16)
    S("RBF", (128, TTW), BF16)
    S("XSQ0", (128, TTW), BF16)
    S("XSQ1", (128, TTW), BF16)
    S("XSQ2", (128, TTW), BF16)
    S("E8", (8, 64), BF16)
    S("SLOPE", (1, c.H))
    S("EPSC", (128, 2))
    S("FEN", (128, 1))
    S("EBIAS", (1, 8))
    S("WRB", (128, c.FC, 8), BF16)
    S("WREP", (64, c.TT, TTW), BF16)
    S("MG", (72, c.TT, TTW), BF16)
    S("MU", (72, c.TT, TTW), BF16)
    S("MD", (72, c.TT, TTW), BF16)
    S("SC", (8, c.chunk), BF16)
    S("SCT", (128, c.TT, TTW // 128, 8))
    S("BSD", (128, TTW // 128, 8))
    S("WTM", (128, TTW // 128, 8))
    S("TOP", (128, TTW // 128, 8))
    S("M1", (128, TTW // 128))
    S("WETB", (8, TTW), BF16)
    S("HS0", (128, TTW), BF16)
    S("HS1", (128, TTW), BF16)
    S("STG0", (128, TTW))
    S("STG1", (128, TTW))
    S("STG2", (128, TTW))
    S("STG3", (128, TTW))
    S("LNB", (128, c.FC, c.TT, TTW), BF16)
    wten = [
        ("WQKB", (128, c.FC, 256), 0),
        ("WQKB1", (128, c.FC, 256), 3 * c.FC * 512),
        ("WVB0", (128, c.FC, 512), c.FC * 512),
        ("WB0", (128, c.FC, 256), 0),
        ("WB1", (128, c.FC, 256), c.FC * 512),
        ("WB2", (128, c.FC, 256), 2 * c.FC * 512),
        ("WB3", (128, c.FC, 256), 3 * c.FC * 512),
        ("WGT0", (128, c.FC + 1, 128), 0),
        ("WGT1", (128, c.FC + 1, 128), (c.FC + 1) * 256),
        ("WUP0", (128, c.FC + 1, 128), 2 * (c.FC + 1) * 256),
        ("WUP1", (128, c.FC + 1, 128), 3 * (c.FC + 1) * 256),
        ("WDN0", (128, c.DC + 1, 128), 0),
        ("WDN1", (128, c.DC + 1, 128), (c.DC + 1) * 256),
        ("LGA", (128, c.FC, 72), 0),
        ("LUA", (128, c.FC, 72), c.FC * 144),
        ("LDA", (128, c.DC, 72), 4 * (c.FC + 1) * 256),
    ]
    wa_bytes = max(o + int(np.prod(sh[1:])) * 2 for _, sh, o in wten)
    wa = nc.alloc_sbuf_tensor("WARENA", [128, wa_bytes], mybir.dt.uint8)
    wabase = nc.lookup_mloc(wa).addr
    wrange = {}
    for nm, sh, o in wten:
        sb[nm] = nc.alloc_sbuf_tensor_at(nm, list(sh), BF16, offset=wabase + o)
        bufs[nm] = Buf(nm)
        wrange[nm] = (o, o + int(np.prod(sh[1:])) * 2)
    woverlap = {}
    for nm, (a0, a1) in wrange.items():
        woverlap[nm] = [bufs[m] for m, (b0, b1) in wrange.items()
                        if m != nm and a0 < b1 and b0 < a1]

    hid_bytes = c.DC * c.chunk * 2
    tenants = [
        ("GT", (128, c.FC, TTW), BF16, None),
        ("ATT", (128, c.H, c.TT, TTW), BF16, None),
        ("X1B", (128, c.FC, TTW), BF16, None),
        ("VTM", (128, c.NG, 512), BF16, None),
        ("QF1", (128, c.chunk), BF16, None),
        ("QD1", (128, c.chunk), BF16, None),
        ("KF1", (128, c.chunk), BF16, None),
        ("KD1", (128, c.NG, 128), BF16, None),
        ("ROWI", (128, BLK), F32, None),
        ("POSM", (128, 2, BLK), F32, None),
    ]
    ten_bytes = 0
    for name, shape, dt, alias in tenants:
        if alias is None:
            n = int(np.prod(shape[1:])) * mybir.dt.size(dt)
            ten_bytes += (n + 31) // 32 * 32
    region_bytes = max(hid_bytes, ten_bytes)
    reg = nc.alloc_sbuf_tensor("OVL", [128, region_bytes], mybir.dt.uint8)
    rbase = nc.lookup_mloc(reg).addr
    cur = rbase
    offs = {}
    for name, shape, dt, alias in tenants:
        if alias is not None:
            offs[name] = offs[alias]
        else:
            offs[name] = cur
            n = int(np.prod(shape[1:])) * mybir.dt.size(dt)
            cur += (n + 31) // 32 * 32
    for name, shape, dt, alias in tenants:
        h = nc.alloc_sbuf_tensor_at(name, list(shape), dt, offset=offs[name])
        sb[name] = h
        bufs[name] = bufs[alias] if alias is not None else Buf(name)
    S("HID", (128, c.DC, c.TT, TTW), BF16, offset=rbase)
    tenant_bufs = [bufs[n] for n, _, _, a in tenants if a is None]

    ps = {}
    for name in ("psA", "psB", "psC", "psD"):
        ps[name] = nc.alloc_psum_tensor(name, [128, TTW], F32)
        bufs[name] = Buf(name)
    ps["psQK"] = nc.alloc_psum_tensor("psQK", [128, TTW], F32)
    bufs["psQK"] = Buf("psQK")
    ps["psAO"] = nc.alloc_psum_tensor("psAO", [128, BLK], F32)
    bufs["psAO"] = Buf("psAO")
    ps["psT"] = nc.alloc_psum_tensor("psT", [128, c.NG, 128], BF16)
    bufs["psT"] = Buf("psT")
    ps["psKV"] = nc.alloc_psum_tensor("psKV", [128, 128], F32)
    bufs["psKV"] = Buf("psKV")

    B = bufs
    T = sb

    def ap(name):
        return T[name][:]

    kvb_b = [Buf(f"kvb{h}") for h in range(c.H)]
    lnbb = [Buf(f"lnb{t}") for t in range(c.TT)]
    attcb = [[Buf(f"att{t}_{o}") for o in range(c.FC)] for t in range(c.TT)]
    x1cb = [Buf(f"x1c{o}") for o in range(c.FC)]
    lnxb = [[Buf(f"lnx{t}_{s}") for s in range(4)] for t in range(c.TT)]
    mgb = [Buf(f"mg{t}") for t in range(c.TT)]
    mub = [Buf(f"mu{t}") for t in range(c.TT)]
    qftb = [[Buf(f"qf{s}_{t}") for t in range(c.TT)] for s in range(2)]
    kftb = [[Buf(f"kf{s}_{t}") for t in range(c.TT)] for s in range(2)]
    qkf_bufs = [b_ for sl in qftb + kftb for b_ in sl]

    def dma(eng, outap, inap, sem, reads=(), writes=(), aliases=()):
        return p.step(eng, lambda e, o=outap, i=inap: e.dma_start(out=o, in_=i),
                      reads=reads, writes=writes, dma=sem, aliases=aliases)

    cinit0 = [("ONESB", onesb), ("ONESF", onesf), ("SLOPE", slope),
              ("IDNB", idnb)]
    cinit = [("IDNF", idnf), ("ROWI", rowi), ("POSM", posm),
             ("ROWKD", rowkd), ("E8", e8), ("EBIAS", ebias), ("NW", normw)]

    def cinit0_fn(e):
        return [e.dma_start(out=ap(nm), in_=src) for nm, src in cinit0]
    p.step("sp", cinit0_fn, writes=[B[nm] for nm, _ in cinit0],
           dma="cinit0", ndma=len(cinit0))

    def cinit_fn(e):
        return [e.dma_start(out=ap(nm), in_=src) for nm, src in cinit]
    p.step("sp", cinit_fn, writes=[B[nm] for nm, _ in cinit],
           dma="cinit", ndma=len(cinit))

    ginit = [("WRB", wr)]

    def ginit_fn(e):
        return [e.dma_start(out=ap(nm), in_=src) for nm, src in ginit]
    p.step("gp", ginit_fn, writes=[B[nm] for nm, _ in ginit],
           dma="ginit", ndma=len(ginit))

    p.step("pe", lambda e: e.matmul(ps["psA"][:, 0:c.H], T["ONESF"][:],
                                    T["SLOPE"][:], start=True, stop=True),
           reads=[B["ONESF"], B["SLOPE"]], writes=[B["psA"]])
    p.step("dve", lambda e: e.tensor_copy(ap("SB"), ps["psA"][:, 0:c.H]),
           reads=[B["psA"]], writes=[B["SB"]])
    p.step("pe", lambda e: e.matmul(ps["psB"][:, 0:8], T["ONESF"][:],
                                    T["EBIAS"][:], start=True, stop=True),
           reads=[B["ONESF"], B["EBIAS"]], writes=[B["psB"]])
    p.step("dve", lambda e: e.tensor_copy(ap("BIAS"), ps["psB"][:, 0:8]),
           reads=[B["psB"]], writes=[B["BIAS"]])
    p.step("act", lambda e: e.activation(ap("BD"), ap("SB"), AF.Exp,
                                         scale=-float(BLK)),
           reads=[B["SB"]], writes=[B["BD"]])
    rowa = T["ROWA"]
    for h in range(c.H):
        p.step("dve", lambda e, h=h: e.tensor_scalar_mul(
            rowa[:, 0, 0:2], T["ROWKD"][:], T["SB"][:, h:h + 1]),
            reads=[B["ROWKD"], B["SB"]], writes=[B["ROWA"]])
        p.step("act", lambda e, h=h: e.activation(
            T["KDC"][:, h, :], rowa[:, 0, 0:2], AF.Exp, scale=-1.0),
            reads=[B["ROWA"]], writes=[B["KDC"]])
    tabs_b = [Buf(f"tab{h}") for h in range(c.H)]

    def emit_table_init():
        s0v = T["STG0"][:].rearrange("p (a b) -> p a b", b=BLK)
        for h in range(c.H):
            stg = ("DT0", "DT1")[h % 2]
            stgq = ("QDC0", "QDC1")[h % 2]
            sB_, sQ = B[stg], B[stgq]
            p.step("dve", lambda e, h=h: e.tensor_scalar_mul(
                s0v, T["POSM"][:], T["SB"][:, h:h + 1]),
                reads=[B["POSM"], B["SB"]], writes=[B["STG0"]])
            p.step("act", lambda e, stg=stg: e.activation(
                T[stg][:], s0v, AF.Exp, scale=-1.0),
                reads=[B["STG0"]], writes=[sB_])
            p.step("dve", lambda e, h=h: e.tensor_scalar_mul(
                T["STG1"][:, 0:BLK], T["ROWI"][:], T["SB"][:, h:h + 1]),
                reads=[B["ROWI"], B["SB"]], writes=[B["STG1"]])
            p.step("act", lambda e, stgq=stgq: e.activation(
                T[stgq][:], T["STG1"][:, 0:BLK], AF.Exp, scale=-1.0),
                reads=[B["STG1"]], writes=[sQ])
            p.step("sp", lambda e, h=h, stg=stg, stgq=stgq: [
                e.dma_start(out=tabs[h, :, 0:2, :], in_=T[stg][:]),
                e.dma_start(out=tabs[h, :, 2, :], in_=T[stgq][:])],
                reads=[sB_, sQ], writes=[tabs_b[h]], dma=f"tabst{h % 2}", ndma=2)
    p.step("dve", lambda e: e.memset(ap("KVB"), 0.0), writes=[B["KVB"]] + kvb_b)
    p.step("dve", lambda e: e.memset(T["EPSC"][:, 0:1], LNEPS), writes=[B["EPSC"]])
    p.step("dve", lambda e: e.memset(T["EPSC"][:, 1:2], RMSEPS), writes=[B["EPSC"]])

    def mm_group(psum_ap, pairs, psum_buf, reads, tag=""):
        n = len(pairs)

        def fn(e, pairs=tuple(pairs), psum_ap=psum_ap, n=n):
            last = None
            for i, (l, r) in enumerate(pairs):
                last = e.matmul(psum_ap, l, r, start=(i == 0), stop=(i == n - 1))
            return last
        return p.step("pe", fn, reads=reads, writes=[psum_buf])

    def ln_rows(sum_buf, sum_ap, sq_buf, sq_ap, epsc, with_mean, mrow, rrow):
        inv = 1.0 / c.D
        if with_mean:
            p.step("dve", lambda e: e.tensor_scalar_mul(
                rowa[:, mrow, :], sum_ap, inv),
                reads=[sum_buf], writes=[B["ROWA"]])
            p.step("act", lambda e: e.activation(
                rowa[:, mrow + 1, :], rowa[:, mrow, :], AF.Square),
                reads=[B["ROWA"]], writes=[B["ROWA"]])
            p.step("dve", lambda e: e.tensor_scalar(
                rowa[:, mrow + 2, :], sq_ap, inv, None, ALU.mult),
                reads=[sq_buf], writes=[B["ROWA"]])
            p.step("dve", lambda e: e.tensor_sub(
                rowa[:, mrow + 2, :], rowa[:, mrow + 2, :], rowa[:, mrow + 1, :]),
                reads=[B["ROWA"]], writes=[B["ROWA"]])
            vrow = mrow + 2
        else:
            p.step("dve", lambda e: e.tensor_scalar_mul(
                rowa[:, mrow + 2, :], sq_ap, inv),
                reads=[sq_buf], writes=[B["ROWA"]])
            vrow = mrow + 2
        p.step("act", lambda e, vrow=vrow: e.activation(
            rowa[:, rrow, :], rowa[:, vrow, :], AF.Ln,
            bias=T["EPSC"][:, epsc:epsc + 1]),
            reads=[B["ROWA"], B["EPSC"]], writes=[B["ROWA"]])
        p.step("act", lambda e: e.activation(
            rowa[:, rrow, :], rowa[:, rrow, :], AF.Exp, scale=-0.5),
            reads=[B["ROWA"]], writes=[B["ROWA"]])

    def rows_cast(mrow, rrow, with_mean=True):
        if with_mean:
            p.step("dve", lambda e: e.tensor_copy(ap("MBF"), rowa[:, mrow, :]),
                   reads=[B["ROWA"]], writes=[B["MBF"]])
        p.step("dve", lambda e: e.tensor_copy(ap("RBF"), rowa[:, rrow, :]),
               reads=[B["ROWA"]], writes=[B["RBF"]])

    x1sb = {(tt, oc): Buf(f"x1s{tt}_{oc}")
            for tt in range(c.TT) for oc in range(c.FC)}
    scr_b = {}

    preloaded = set()

    def wload(ci, bufname, src_f32, scr_slice, key, q="gp"):
        if key in preloaded:
            preloaded.discard(key)
            return
        if key not in scr_b:
            scr_b[key] = Buf(f"scr{key}")
        al = woverlap[bufname]
        dstap = ap(bufname)
        n1 = dstap.shape[1]
        ns = 4 if ci == 0 else 2
        bounds = sorted(set(n1 * k // ns for k in range(ns + 1)))
        segs = [(a, b) for a, b in zip(bounds, bounds[1:]) if b > a]
        if ci == 0:
            def ldfn(e, o=dstap, i=src_f32, segs=segs):
                return [e.dma_start(out=o[:, a:b], in_=i[:, a:b])
                        for a, b in segs]
            p.step("gp", ldfn, writes=[B[bufname]], aliases=al, dma=bufname,
                   ndma=len(segs))

            def tefn(e, o=scr_slice, i=dstap, segs=segs):
                return [e.dma_start(out=o[:, a:b], in_=i[:, a:b])
                        for a, b in segs]
            p.step("sp", tefn, reads=[B[bufname]], writes=[scr_b[key]],
                   dma=bufname + "s", ndma=len(segs))
        else:
            def rlfn(e, o=dstap, i=scr_slice, segs=segs):
                return [e.dma_start(out=o[:, a:b], in_=i[:, a:b])
                        for a, b in segs]
            p.step(q, rlfn, reads=[scr_b[key]], writes=[B[bufname]],
                   aliases=al, dma=bufname + ("h" if q == "sp" else ""),
                   ndma=len(segs))
    for ci in range(c.NC):
        p.step("dve", lambda e: e.memset(T["FEN"][:], 0.0),
               writes=[B["HID"], B["FEN"]] + tenant_bufs + qkf_bufs
               + [bb for tl in attcb for bb in tl])
        def ln1_stat_unit(tt, fc, ci=ci):
            xs = ("XSQ0", "XSQ1")[fc % 2]
            slot = lnxb[tt][fc % 4]
            dst = T["LNB"][:, fc, tt, :]
            if ci == 0:
                sslot = ("STG0", "STG1", "STG2", "STG3")[(tt * c.FC + fc) % 4]
                dma("sp", ap(sslot), xt[ci, tt, :, fc, :], sslot,
                    writes=[B[sslot]])
                p.step("act", lambda e, dst=dst, sslot=sslot: e.activation(
                    dst, ap(sslot), AF.Copy),
                    reads=[B[sslot]], writes=[slot])
            else:
                dma("gp", dst, xt[ci, tt, :, fc, :], f"lnx{tt}_{fc % 4}",
                    writes=[slot])
            p.step("act", lambda e, dst=dst, xs=xs: e.activation(
                T[xs][:], dst, AF.Square),
                reads=[slot], writes=[B[xs]])

            def stfn(e, dst=dst, xs=xs, fc=fc):
                e.matmul(ps["psQK"][:], T["ONESB"][:], dst,
                         start=(fc == 0), stop=(fc == c.FC - 1))
                return e.matmul(ps["psC"][:], T["ONESB"][:], T[xs][:],
                                start=(fc == 0), stop=(fc == c.FC - 1))
            p.step("pe", stfn, reads=[slot, B[xs], B["ONESB"]],
                   writes=[B["psQK"], B["psC"]])

        def ln1_rows(tt, ci=ci):
            ln_rows(B["psQK"], ps["psQK"][:], B["psC"], ps["psC"][:],
                    0, True, mrow=0, rrow=3)
            p.step("dve", lambda e: e.memset(T["FEN"][:], 0.0),
                   reads=lnxb[tt], writes=[lnbb[tt], B["FEN"]])
            rows_cast(0, 3)

        def ln1_apply_piece(tt, a, ci=ci):
            fq = c.FC // 4
            dst = T["LNB"][:, a * fq:(a + 1) * fq, tt, :]
            mb = T["MBF"][:].unsqueeze(1).broadcast_to([128, fq, TTW])
            rb = T["RBF"][:].unsqueeze(1).broadcast_to([128, fq, TTW])
            p.step("dve", lambda e, dst=dst, mb=mb: e.tensor_tensor(
                dst, dst, mb, ALU.subtract),
                reads=[B["MBF"], lnbb[tt]], writes=[lnbb[tt]])
            p.step("dve", lambda e, dst=dst, rb=rb: e.tensor_tensor(
                dst, dst, rb, ALU.mult),
                reads=[B["RBF"], lnbb[tt]], writes=[lnbb[tt]])

        def ln1_apply(tt, ci=ci):
            ln1_rows(tt, ci)
            for a in range(4):
                ln1_apply_piece(tt, a, ci)
        if ci == 0:
            for pk, bn, sf, ss in ((("wqk", 0), "WQKB1", wqk[0], wqks[0]),
                                   (("wv", 0), "WVB0", wv[0], wvs[0]),
                                   (("wqk", 1), "WQKB", wqk[1], wqks[1])):
                wload(0, bn, sf, ss, pk)
                preloaded.add(pk)
            for tt_ in range(c.TT):
                for fc_ in range(c.FC):
                    ln1_stat_unit(tt_, fc_, 0)
                ln1_apply(tt_, 0)
            emit_table_init()


        def vproj(g):
            wload(ci, "WVB0", wv[g], wvs[g], ("wv", g))
            wvb = T["WVB0"]
            wvbB = B["WVB0"]
            for tc in range(c.NG):
                tt0, o0 = divmod(tc * 128, TTW)
                pairs = [(T["LNB"][:, fc, tt0, o0:o0 + 128], wvb[:, fc, :])
                         for fc in range(c.FC)]
                pb = "psA" if tc % 2 == 0 else "psB"
                mm_group(ps[pb][:], pairs, B[pb], [lnbb[tt0], wvbB])
                p.step("act", lambda e, tc=tc, pb=pb: e.activation(
                    T["VTM"][:, tc, :], ps[pb][:], AF.Silu),
                    reads=[B[pb]], writes=[B["VTM"]])

        qkv_banks = [ps["psQK"][:].rearrange("p (a b) -> p a b", b=BLK),
                     ps["psC"][:].rearrange("p (a b) -> p a b", b=BLK)]
        qkv_bufs = [B["psQK"], B["psC"]]
        ao_banks = [ps["psAO"][:], ps["psD"][:, 0:BLK]]
        ao_bufs = [B["psAO"], B["psD"]]
        qkd_names = ["QKD", "QKD2", "QKD3"]

        def emit_qk(h, b):
            s = h % 2
            i0_ = b * BLK
            bank = qkv_banks[b % 2]

            def qkfn(e, b=b, i0_=i0_, bank=bank, s=s):
                e.matmul(bank[:, 0, :],
                         T[f"KF{s}"][:, (2 * b) * 128:(2 * b + 1) * 128],
                         T[f"QF{s}"][:, i0_:i0_ + BLK], start=True, stop=True)
                return e.matmul(bank[:, 1, :],
                                T[f"KF{s}"][:, (2 * b + 1) * 128:(2 * b + 2) * 128],
                                T[f"QF{s}"][:, i0_:i0_ + BLK], start=True, stop=True)
            tti = i0_ // TTW
            p.step("pe", qkfn, reads=[kftb[s][tti], qftb[s][tti]],
                   writes=[qkv_bufs[b % 2]])
            qkd = qkd_names[(h * c.NB + b) % 3]
            dtb_ = ("DT0", "DT1")[s]
            p.step("dve", lambda e, dtb_=dtb_, qkd=qkd, bank=bank:
                   e.tensor_tensor(T[qkd][:], bank, T[dtb_][:], ALU.mult),
                   reads=[qkv_bufs[b % 2], B[dtb_]], writes=[B[qkd]])

        def proj(h):
            s = h % 2
            dtb, qdc = ("DT0", "QDC0") if s == 0 else ("DT1", "QDC1")
            p.step("gp", lambda e, h=h, dtb=dtb, qdc=qdc: [
                e.dma_start(out=T[dtb][:], in_=tabs[h, :, 0:2, :]),
                e.dma_start(out=T[qdc][:], in_=tabs[h, :, 2, :])],
                reads=[tabs_b[h]], writes=[B[dtb], B[qdc]],
                dma=dtb, ndma=2)
            wqn = ("WQKB1", "WQKB")[h % 2]
            wload(ci, wqn, wqk[h], wqks[h], ("wqk", h), q="sp")
            for tt in range(c.TT):
                qpairs = [(T[wqn][:, fc, 0:128], T["LNB"][:, fc, tt, :])
                          for fc in range(c.FC)]
                kpairs = [(T[wqn][:, fc, 128:256], T["LNB"][:, fc, tt, :])
                          for fc in range(c.FC)]
                mm_group(ps["psA"][:], qpairs, B["psA"], [B[wqn], lnbb[tt]])
                mm_group(ps["psB"][:], kpairs, B["psB"], [B[wqn], lnbb[tt]])
                p.step("act", lambda e, tt=tt, s=s: e.activation(
                    T[f"QF{s}"][:, tt * TTW:(tt + 1) * TTW], ps["psA"][:], AF.Silu),
                    reads=[B["psA"]], writes=[qftb[s][tt]],
                    aliases=[B[f"QF{s}"]])
                p.step("act", lambda e, tt=tt, s=s: e.activation(
                    T[f"KF{s}"][:, tt * TTW:(tt + 1) * TTW], ps["psB"][:], AF.Silu),
                    reads=[B["psB"]], writes=[kftb[s][tt]],
                    aliases=[B[f"KF{s}"]])
                ngh = c.NG // c.TT

                def tf(e, tt=tt, s=s, ngh=ngh):
                    last = None
                    for jt in range(tt * ngh, (tt + 1) * ngh):
                        last = e.transpose(ps["psT"][:, jt, :],
                                           T[f"KF{s}"][:, jt * 128:(jt + 1) * 128],
                                           T["IDNB"][:])
                    return last
                p.step("pe", tf, reads=[kftb[s][tt], B["IDNB"]],
                       writes=[B["psT"]])
            qdb = T[qdc][:].unsqueeze(1).broadcast_to([128, c.NB, BLK])
            p.step("dve", lambda e, qdb=qdb, s=s: e.tensor_tensor(
                T[f"QD{s}"][:].rearrange("p (nb i) -> p nb i", i=BLK),
                T[f"QF{s}"][:].rearrange("p (nb i) -> p nb i", i=BLK),
                qdb, ALU.mult),
                reads=qftb[s][:c.TT] + [B[qdc]], writes=[B[f"QD{s}"]])

            for par in range(2):
                src_ = ps["psT"][:, par::2, :]
                dst_ = T[f"KD{s}"][:, par::2, :]
                p.step("dve", lambda e, src_=src_, dst_=dst_, h=h, par=par:
                       e.tensor_scalar_mul(dst_, src_, T["KDC"][:, h, par:par + 1]),
                       reads=[B["psT"], B["KDC"]], writes=[B[f"KD{s}"]])
            if h % 4 == 0:
                emit_qk(h, 0)
                if c.NB > 1:
                    emit_qk(h, 1)

        def blocks(h):
            s = h % 2
            j = h % 4
            dtb = ("DT0", "DT1")[s]
            for b in range(c.NB):
                i0_ = b * BLK
                qkd = qkd_names[(h * c.NB + b) % 3]
                aob = ao_banks[b % 2]

                def aofn(e, b=b, i0_=i0_, h=h, j=j, qkd=qkd, aob=aob, s=s):
                    e.matmul(aob, T["KVB"][:, h, :],
                             T[f"QD{s}"][:, i0_:i0_ + BLK], start=True, stop=False)
                    e.matmul(aob, T["VTM"][:, 2 * b, j * 128:(j + 1) * 128],
                             T[qkd][:, 0, :], start=False, stop=False)
                    return e.matmul(aob,
                                    T["VTM"][:, 2 * b + 1, j * 128:(j + 1) * 128],
                                    T[qkd][:, 1, :], start=False, stop=True)
                p.step("pe", aofn, reads=[kvb_b[h], B[f"QD{s}"], B["VTM"], B[qkd]],
                       writes=[ao_bufs[b % 2]])
                tt0 = b // (TTW // BLK)
                half = b % (TTW // BLK)
                p.step("act", lambda e, h=h, tt0=tt0, half=half, aob=aob:
                       e.activation(
                           T["ATT"][:, h, tt0, half * BLK:(half + 1) * BLK],
                           aob, AF.Copy),
                       reads=[ao_bufs[b % 2]], writes=[attcb[tt0][h]])

                def kvfn(e, b=b, j=j, s=s):
                    e.matmul(ps["psKV"][:], T[f"KD{s}"][:, 2 * b, :],
                             T["VTM"][:, 2 * b, j * 128:(j + 1) * 128],
                             start=True, stop=False)
                    return e.matmul(ps["psKV"][:], T[f"KD{s}"][:, 2 * b + 1, :],
                                    T["VTM"][:, 2 * b + 1, j * 128:(j + 1) * 128],
                                    start=False, stop=True)
                p.step("pe", kvfn, reads=[B[f"KD{s}"], B["VTM"]], writes=[B["psKV"]])
                p.step("dve", lambda e, h=h: e.scalar_tensor_tensor(
                    T["KVB"][:, h, :], T["KVB"][:, h, :], T["BD"][:, h:h + 1],
                    ps["psKV"][:], ALU.mult, ALU.add),
                    reads=[B["psKV"], B["BD"], kvb_b[h]], writes=[kvb_b[h]])
                if b + 2 < c.NB:
                    emit_qk(h, b + 2)
                elif h + 1 < c.H and (h + 1) % 4 != 0:
                    emit_qk(h + 1, b + 2 - c.NB)
        proj(0)
        vproj(0)
        for h in range(c.H):
            nh = h + 1
            if nh < c.H and nh % 4 != 0:
                proj(nh)
                blocks(h)
            elif nh < c.H:
                blocks(h)
                proj(nh)
                vproj(nh // 4)
            else:
                blocks(h)

        efg_seq = [0]

        def emit_gate(tt):
            for oc in range(c.FC):
                if oc % 2 == 0:
                    wbn = ("WB0", "WB1", "WB2", "WB3")[efg_seq[0] % 4]
                    efg_seq[0] += 1
                    wload(ci, wbn, wg[oc // 2], wgs[oc // 2], ("wg", oc // 2))
                    emit_gate.cur = wbn
                wbn = emit_gate.cur
                sub = oc % 2
                xs = ("XSQ0", "XSQ1")[oc % 2]
                p.step("act", lambda e, xs=xs, oc=oc, tt=tt: e.activation(
                    T[xs][:], T["ATT"][:, oc, tt, :], AF.Square),
                    reads=[attcb[tt][oc]], writes=[B[xs]])
                pairs = [(T[wbn][:, fc, sub * 128:(sub + 1) * 128],
                          T["LNB"][:, fc, tt, :]) for fc in range(c.FC)]
                pb = ("psA", "psB", "psC")[oc % 3]
                mm_group(ps[pb][:], pairs, B[pb], [B[wbn], lnbb[tt]])
                p.step("pe", lambda e, xs=xs, oc=oc: e.matmul(
                    ps["psD"][:], T["ONESB"][:], T[xs][:],
                    start=(oc == 0), stop=(oc == c.FC - 1)),
                    reads=[B[xs], B["ONESB"]], writes=[B["psD"]])
                p.step("act", lambda e, oc=oc, pb=pb: e.activation(
                    T["GT"][:, oc, :], ps[pb][:], AF.Sigmoid),
                    reads=[B[pb]], writes=[B["GT"]])
                p.step("dve", lambda e, oc=oc, tt=tt: e.scalar_tensor_tensor(
                    T["ATT"][:, oc, tt, :], T["ATT"][:, oc, tt, :],
                    T["NW"][:, oc:oc + 1], T["GT"][:, oc, :],
                    ALU.mult, ALU.mult),
                    reads=[B["GT"], attcb[tt][oc]], writes=[attcb[tt][oc]])

        def emit_rows_p(tt, defer=False):
            p.step("dve", lambda e: e.tensor_scalar_mul(
                rowa[:, 0, :], ps["psD"][:], 1.0 / c.D),
                reads=[B["psD"]], writes=[B["ROWA"]])
            p.step("act", lambda e: e.activation(
                rowa[:, 1, :], rowa[:, 0, :], AF.Ln,
                bias=T["EPSC"][:, 1:2]),
                reads=[B["ROWA"], B["EPSC"]], writes=[B["ROWA"]])
            p.step("act", lambda e: e.activation(
                rowa[:, 1, :], rowa[:, 1, :], AF.Exp, scale=-0.5),
                reads=[B["ROWA"]], writes=[B["ROWA"]])
            rows_cast(0, 1, with_mean=False)
            nq = c.H // 4
            pieces = []
            for a in range(4):
                attv = T["ATT"][:, a * nq:(a + 1) * nq, tt, :]
                rbb = T["RBF"][:].unsqueeze(1).broadcast_to([128, nq, TTW])

                def mk(attv=attv, rbb=rbb, a=a, tt=tt):
                    p.step("dve", lambda e, attv=attv, rbb=rbb: e.tensor_tensor(
                        attv, attv, rbb, ALU.mult), reads=[B["RBF"]],
                        writes=attcb[tt][a * nq:(a + 1) * nq])
                pieces.append(mk)
            if not defer:
                for mk in pieces:
                    mk()
                return []
            return pieces

        def emit_out(tt, inject=(), inj_start=4, inj_step=2, defer_ln2=False):
            inj = list(inject)

            def emit_l2stat(oc):
                xs = ("XSQ0", "XSQ1", "XSQ2")[oc % 3]

                def l2fn(e, oc=oc, xs=xs):
                    e.matmul(ps["psQK"][:], T["ONESB"][:], T["X1B"][:, oc, :],
                             start=(oc == 0), stop=(oc == c.FC - 1))
                    return e.matmul(ps["psD"][:], T["ONESB"][:], T[xs][:],
                                    start=(oc == 0), stop=(oc == c.FC - 1))
                p.step("pe", l2fn, reads=[x1cb[oc], B[xs], B["ONESB"]],
                       writes=[B["psQK"], B["psD"]])
            for oc in range(c.FC):
                if inj and oc >= inj_start and (oc - inj_start) % inj_step == 0:
                    inj.pop(0)()
                if oc % 2 == 0:
                    wbn = ("WB0", "WB1", "WB2", "WB3")[efg_seq[0] % 4]
                    efg_seq[0] += 1
                    wload(ci, wbn, wo[oc // 2], wos[oc // 2], ("wo", oc // 2))
                    emit_out.cur = wbn
                wbn = emit_out.cur
                sub = oc % 2
                stg = ("STG0", "STG1", "STG2", "STG3")[oc % 4]
                if ci == 0:
                    dma("sp", ap(stg), xt[ci, tt, :, oc, :], stg,
                        writes=[B[stg]])
                else:
                    dma("gp", ap(stg), xt[ci, tt, :, oc, :], stg + "g",
                        writes=[B[stg]])
                pairs = [(T[wbn][:, fc, sub * 128:(sub + 1) * 128],
                          T["ATT"][:, fc, tt, :]) for fc in range(c.FC)]
                pb = ("psA", "psB", "psC")[oc % 3]
                mm_group(ps[pb][:], pairs, B[pb], [B[wbn]] + attcb[tt])
                p.step("dve", lambda e, stg=stg, pb=pb: e.tensor_tensor(
                    ap(stg), ap(stg), ps[pb][:], ALU.add),
                    reads=[B[pb], B[stg]], writes=[B[stg]])
                p.step("act", lambda e, stg=stg, oc=oc: e.activation(
                    T["X1B"][:, oc, :], ap(stg), AF.Copy),
                    reads=[B[stg]], writes=[x1cb[oc]])
                xs = ("XSQ0", "XSQ1", "XSQ2")[oc % 3]
                p.step("act", lambda e, stg=stg, xs=xs: e.activation(
                    T[xs][:], ap(stg), AF.Square),
                    reads=[B[stg]], writes=[B[xs]])
                dma("sp", x1s[ci, tt, :, oc, :], ap(stg), stg, reads=[B[stg]],
                    writes=[x1sb[(tt, oc)]])
                if oc > 1:
                    emit_l2stat(oc - 2)
            for fn in inj:
                fn()
            inj = []
            emit_l2stat(c.FC - 2)
            emit_l2stat(c.FC - 1)
            ln_rows(B["psQK"], ps["psQK"][:], B["psD"], ps["psD"][:],
                    0, True, mrow=0, rrow=3)
            rows_cast(0, 3)
            fq = c.FC // 4
            ln2p = []
            for a in range(4):
                dsta = T["LNB"][:, a * fq:(a + 1) * fq, tt, :]
                x1a = T["X1B"][:, a * fq:(a + 1) * fq, :]
                mba = T["MBF"][:].unsqueeze(1).broadcast_to([128, fq, TTW])
                rba = T["RBF"][:].unsqueeze(1).broadcast_to([128, fq, TTW])

                def mksub(dsta=dsta, x1a=x1a, mba=mba, a=a, tt=tt):
                    p.step("dve", lambda e, dsta=dsta, x1a=x1a, mba=mba:
                           e.tensor_tensor(dsta, x1a, mba, ALU.subtract),
                           reads=x1cb[a * fq:(a + 1) * fq] + [B["MBF"]],
                           writes=[lnbb[tt]])

                def mkmul(dsta=dsta, rba=rba, tt=tt):
                    p.step("dve", lambda e, dsta=dsta, rba=rba:
                           e.tensor_tensor(dsta, dsta, rba, ALU.mult),
                           reads=[B["RBF"], lnbb[tt]], writes=[lnbb[tt]])
                ln2p += [mksub, mkmul]
            if defer_ln2:
                return ln2p
            for fn in ln2p:
                fn()
        emit_gate(0)
        emit_rows_p(0)
        if c.TT > 1:
            emit_gate(1)
            pieces = emit_rows_p(1, defer=True)
            ln2p = emit_out(0, inject=pieces, defer_ln2=True)
            emit_out(1, inject=ln2p, inj_start=0, inj_step=1)
        else:
            emit_out(0)

        dma("gp", ap("LGA"), lga, "LGA", writes=[B["LGA"]],
            aliases=woverlap["LGA"])
        dma("gp", ap("LUA"), lua, "LUA", writes=[B["LUA"]],
            aliases=woverlap["LUA"])
        ngr = TTW // 128
        abank = (("psA", "psB"), ("psC", "psD"))

        def prep_router(tt):
            rpairs = [(T["WRB"][:, fc, :], T["LNB"][:, fc, tt, :])
                      for fc in range(c.FC)]
            mm_group(ps["psC"][0:8, :], rpairs, B["psC"], [B["WRB"], lnbb[tt]])
            p.step("act", lambda e, tt=tt: e.activation(
                T["SC"][:, tt * TTW:(tt + 1) * TTW], ps["psC"][0:8, :], AF.Sigmoid),
                reads=[B["psC"]], writes=[B["SC"]])

        def prep_top2(tt):
            def scfn(e, tt=tt):
                last = None
                for gidx in range(ngr):
                    last = e.transpose(
                        ps["psT"][:, 0, gidx * 8:(gidx + 1) * 8],
                        T["SC"][0:8, tt * TTW + gidx * 128: tt * TTW + (gidx + 1) * 128],
                        T["IDNB"][0:8, 0:8])
                return last
            p.step("pe", scfn, reads=[B["SC"], B["IDNB"]], writes=[B["psT"]])
            p.step("dve", lambda e, tt=tt: e.tensor_copy(
                T["SCT"][:, tt], ps["psT"][:, 0, 0:ngr * 8].rearrange(
                    "p (g e) -> p g e", e=8)),
                reads=[B["psT"]], writes=[B["SCT"]])
            bb = T["BIAS"][:].unsqueeze(1).broadcast_to([128, ngr, 8])
            p.step("dve", lambda e, tt=tt, bb=bb: e.tensor_tensor(
                T["BSD"][:], T["SCT"][:, tt], bb, ALU.add),
                reads=[B["SCT"], B["BIAS"]], writes=[B["BSD"]])
            p.step("dve", lambda e: e.reduce_max(T["M1"][:], T["BSD"][:], axis=AX.X),
                   reads=[B["BSD"]], writes=[B["M1"]])
            m1b = T["M1"][:].unsqueeze(2).broadcast_to([128, ngr, 8])
            p.step("dve", lambda e, m1b=m1b: e.tensor_tensor(
                T["TOP"][:], T["BSD"][:], m1b, ALU.is_ge),
                reads=[B["BSD"], B["M1"]], writes=[B["TOP"]])
            p.step("dve", lambda e: e.scalar_tensor_tensor(
                T["BSD"][:], T["TOP"][:], -1e9, T["BSD"][:], ALU.mult, ALU.add),
                reads=[B["TOP"], B["BSD"]], writes=[B["BSD"]])
            p.step("dve", lambda e: e.reduce_max(T["M1"][:], T["BSD"][:], axis=AX.X),
                   reads=[B["BSD"]], writes=[B["M1"]])
            p.step("dve", lambda e, m1b=m1b: e.tensor_tensor(
                T["WTM"][:], T["BSD"][:], m1b, ALU.is_ge),
                reads=[B["BSD"], B["M1"]], writes=[B["WTM"]])
            p.step("dve", lambda e: e.tensor_add(
                T["TOP"][:], T["TOP"][:], T["WTM"][:]),
                reads=[B["WTM"], B["TOP"]], writes=[B["TOP"]])
            p.step("dve", lambda e, tt=tt: e.tensor_tensor(
                T["WTM"][:], T["TOP"][:], T["SCT"][:, tt], ALU.mult),
                reads=[B["TOP"], B["SCT"]], writes=[B["WTM"]])
            p.step("dve", lambda e: e.reduce_sum(T["M1"][:], T["WTM"][:], axis=AX.X),
                   reads=[B["WTM"]], writes=[B["M1"]])
            p.step("dve", lambda e: e.tensor_scalar_add(
                T["M1"][:], T["M1"][:], TOPEPS),
                reads=[B["M1"]], writes=[B["M1"]])
            p.step("dve", lambda e: e.reciprocal(T["M1"][:], T["M1"][:]),
                   reads=[B["M1"]], writes=[B["M1"]])
            p.step("dve", lambda e, m1b=m1b, tt=tt: e.tensor_tensor(
                T["WTM"][:], T["WTM"][:], m1b, ALU.mult),
                reads=[B["M1"], B["WTM"]], writes=[B["WTM"]])
        def prep_wet(tt):
            def wtfn(e):
                last = None
                for gidx in range(ngr):
                    last = e.transpose(
                        ps["psQK"][0:8, gidx * 128:(gidx + 1) * 128],
                        T["WTM"][:, gidx, :], T["IDNF"][:])
                return last
            p.step("pe", wtfn, reads=[B["WTM"], B["IDNF"]], writes=[B["psQK"]])
            p.step("dve", lambda e: e.tensor_copy(T["WETB"][:], ps["psQK"][0:8, :]),
                   reads=[B["psQK"]], writes=[B["WETB"]])
            p.step("pe", lambda e: e.matmul(
                ps["psQK"][0:64, :], T["E8"][:], T["WETB"][:], start=True, stop=True),
                reads=[B["E8"], B["WETB"]], writes=[B["psQK"]])
            p.step("dve", lambda e, tt=tt: e.tensor_copy(
                T["WREP"][:, tt, :], ps["psQK"][0:64, :]),
                reads=[B["psQK"]], writes=[B["WREP"]])

        def prep_a(tt):
            ga, ua = abank[tt]
            gpairs = [(T["LGA"][:, fc, :], T["LNB"][:, fc, tt, :])
                      for fc in range(c.FC)]
            upairs = [(T["LUA"][:, fc, :], T["LNB"][:, fc, tt, :])
                      for fc in range(c.FC)]
            mm_group(ps[ga][0:72, :], gpairs, B[ga], [B["LGA"], lnbb[tt]])
            mm_group(ps[ua][0:72, :], upairs, B[ua], [B["LUA"], lnbb[tt]])

        def prep_m(tt):
            ga, ua = abank[tt]
            p.step("dve", lambda e, tt=tt, ga=ga: e.tensor_tensor(
                T["MG"][0:64, tt, :], ps[ga][0:64, :], T["WREP"][:, tt, :],
                ALU.mult), reads=[B[ga], B["WREP"]], writes=[mgb[tt]])
            p.step("dve", lambda e, tt=tt, ga=ga: e.tensor_scalar_mul(
                T["MG"][64:72, tt, :], ps[ga][64:72, :], 0.125),
                reads=[B[ga]], writes=[mgb[tt]])
            p.step("dve", lambda e, tt=tt, ua=ua: e.tensor_tensor(
                T["MU"][0:64, tt, :], ps[ua][0:64, :], T["WREP"][:, tt, :],
                ALU.mult), reads=[B[ua], B["WREP"]], writes=[mub[tt]])
            p.step("dve", lambda e, tt=tt, ua=ua: e.tensor_scalar_mul(
                T["MU"][64:72, tt, :], ps[ua][64:72, :], 0.125),
                reads=[B[ua]], writes=[mub[tt]])
        prep_router(0)
        prep_a(0)
        prep_top2(0)
        if c.TT > 1:
            prep_router(1)
            prep_a(1)
        prep_wet(0)
        prep_m(0)
        if c.TT > 1:
            prep_top2(1)
        p.step("dve", lambda e: e.memset(T["FEN"][:], 0.0),
               writes=[B["HID"], B["FEN"]] + tenant_bufs + qkf_bufs
               + [bb for tl in attcb for bb in tl])
        dma("gp", ap("LDA"), lda, "LDA", writes=[B["LDA"]],
            aliases=woverlap["LDA"])
        for dc in range(c.DC):
            wgb = ("WGT0", "WGT1")[dc % 2]
            wub = ("WUP0", "WUP1")[dc % 2]
            wload(ci, wgb, wgate[dc], wgates[dc], ("wgate", dc))
            wload(ci, wub, wup[dc], wups[dc], ("wup", dc))
            for tt in range(c.TT):
                if dc == 0 and tt == 1:
                    prep_wet(1)
                    prep_m(1)
                gp_ = ("psA", "psC")[tt]
                up_ = ("psB", "psD")[tt]
                gpairs = [(T[wgb][:, fc, :], T["LNB"][:, fc, tt, :])
                          for fc in range(c.FC)]
                gpairs.append((T[wgb][0:72, c.FC, :], T["MG"][:, tt, :]))
                upairs = [(T[wub][:, fc, :], T["LNB"][:, fc, tt, :])
                          for fc in range(c.FC)]
                upairs.append((T[wub][0:72, c.FC, :], T["MU"][:, tt, :]))
                mm_group(ps[gp_][:], gpairs, B[gp_], [B[wgb], lnbb[tt], mgb[tt]])
                mm_group(ps[up_][:], upairs, B[up_], [B[wub], lnbb[tt], mub[tt]])
                hs = ("HS0", "HS1")[tt]
                p.step("act", lambda e, hs=hs, gp_=gp_: e.activation(
                    ap(hs), ps[gp_][:], AF.Silu), reads=[B[gp_]], writes=[B[hs]])
                p.step("dve", lambda e, hs=hs, up_=up_, dc=dc, tt=tt: e.tensor_tensor(
                    T["HID"][:, dc, tt, :], ps[up_][:], ap(hs), ALU.mult),
                    reads=[B[up_], B[hs]], writes=[B["HID"]])

        for tt in range(c.TT):
            dpairs = [(T["LDA"][:, fc, :], T["HID"][:, fc, tt, :])
                      for fc in range(c.DC)]
            mm_group(ps["psA"][0:72, :], dpairs, B["psA"], [B["LDA"], B["HID"]])
            p.step("dve", lambda e, tt=tt: e.tensor_tensor(
                T["MD"][0:64, tt, :], ps["psA"][0:64, :], T["WREP"][:, tt, :],
                ALU.mult), reads=[B["psA"], B["WREP"]], writes=[B["MD"]])
            p.step("dve", lambda e, tt=tt: e.tensor_scalar_mul(
                T["MD"][64:72, tt, :], ps["psA"][64:72, :], 0.125),
                reads=[B["psA"]], writes=[B["MD"]])
        nxt = ci + 1 < c.NC
        ride = []
        if nxt and c.TT > 1:
            for att in range(c.TT):
                for fc in range(c.FC):
                    ride.append(lambda att=att, fc=fc:
                                ln1_stat_unit(att, fc, ci + 1))
                ride.append(lambda att=att: ln1_rows(att, ci + 1))
                for a in range(4):
                    ride.append(lambda att=att, a=a:
                                ln1_apply_piece(att, a, ci + 1))
        rpos = [0]
        for oc in range(c.FC):
            wdb = ("WDN0", "WDN1")[oc % 2]
            wload(ci, wdb, wdown[oc], wdowns[oc], ("wdown", oc))
            for tt in range(c.TT):
                stg = ("STG2", "STG3")[tt]
                dma("sp", ap(stg), x1s[ci, tt, :, oc, :], stg,
                    reads=[x1sb[(tt, oc)]], writes=[B[stg]])
                pairs = [(T[wdb][:, fc, :], T["HID"][:, fc, tt, :])
                         for fc in range(c.DC)]
                pairs.append((T[wdb][0:72, c.DC, :], T["MD"][:, tt, :]))
                pb = ("psA", "psB", "psD")[(2 * oc + tt) % 3]
                mm_group(ps[pb][:], pairs, B[pb], [B[wdb], B["HID"], B["MD"]])
                p.step("dve", lambda e, stg=stg, pb=pb: e.tensor_tensor(
                    ap(stg), ap(stg), ps[pb][:], ALU.add),
                    reads=[B[pb], B[stg]], writes=[B[stg]])
                dma("sp", out[ci, tt, :, oc, :], ap(stg), stg, reads=[B[stg]])
            if nxt and oc == c.FC - 4:
                wload(ci + 1, "WQKB1", wqk[0], wqks[0], ("wqk", 0), q="sp")
                preloaded.add(("wqk", 0))
            if nxt:
                if c.TT > 1:
                    take = ride[rpos[0]:rpos[0] + 4]
                    rpos[0] += len(take)
                    for it in take:
                        it()
                else:
                    ln1_stat_unit(0, oc, ci + 1)
                    if oc == c.FC - 1:
                        ln1_apply(0, ci + 1)

    p.emit()
    nc.finalize()
    return nc


def kernel(**inputs):
    x = np.asarray(inputs["L0_attn_hidden_states"])
    Bsz, NT, D = x.shape
    H = int(np.asarray(inputs["L0_attn_slope_rate"]).shape[0])
    DFF = int(np.asarray(inputs["L0_ffn_W_gate"]).shape[1])
    cfg = Cfg(D=D, H=H, NT=NT, DFF=DFF, chunk=1024)
    shared, xs = prep_inputs(cfg, inputs)
    nc = build_nc(cfg)
    in_maps = [dict(shared, xt=xs[b]) for b in range(Bsz)]
    res = run_bass_kernel_spmd(nc, in_maps, core_ids=list(range(Bsz)))
    outs = [r["out"] for r in res.results]
    return unshard_out(cfg, outs)
